# version 12
# baseline (speedup 1.0000x reference)
import numpy as np
import ml_dtypes
from contextlib import ExitStack
import concourse.bass as bass
import concourse.mybir as mybir
from concourse.bass_utils import run_bass_kernel_spmd

F32 = mybir.dt.float32
BF16 = mybir.dt.bfloat16
AF = mybir.ActivationFunctionType
ALU = mybir.AluOpType
AX = mybir.AxisListType
BFNP = ml_dtypes.bfloat16

NTOK = 2048
D = 1024
NCORES = 8

_PHASES = ("l0attn", "l0ffn", "l1attn", "moe")
_L0CUT = "full"
_BAR = ""
_FFNCUT = "full"
_CASTENG = None
_DBG = False


class Res:
    __slots__ = ("w", "r")

    def __init__(self):
        self.w = None
        self.r = []


class Em:
    ENGS = ("pe", "act", "dve", "pool", "sp")

    def __init__(self, nc, ctx):
        self.nc = nc
        self.streams = {k: [] for k in self.ENGS}
        self.cnt = {k: 0 for k in self.ENGS}
        self.sat = {k: {} for k in self.ENGS}
        self.semh = {k: ctx.enter_context(nc.semaphore(f"sem_{k}")) for k in self.ENGS}

    def op(self, eng, fn, reads=(), writes=(), dma=False):
        evs = {}

        def need(ev):
            if ev is None:
                return
            s, v = ev
            if eng == "pe" and s == "pe":
                return
            if s in ("sp", "pool"):
                v = self.cnt[s]
            if self.sat[eng].get(s, 0) >= v:
                return
            if evs.get(s, 0) < v:
                evs[s] = v

        for r in reads:
            need(r.w)
        for w in writes:
            if w.r:
                for ev in w.r:
                    need(ev)
            else:
                need(w.w)
        inc = 16 if dma else 1
        self.cnt[eng] += inc
        ev = (eng, self.cnt[eng])
        waits = sorted(evs.items())
        self.streams[eng].append((waits, fn, eng, inc))
        for s, v in waits:
            self.sat[eng][s] = max(self.sat[eng].get(s, 0), v)
        for w in writes:
            w.w = ev
            w.r = []
        for r in reads:
            r.r.append(ev)
        return ev

    def barrier(self):
        snap = dict(self.cnt)
        for eng in self.ENGS:
            waits = sorted((s, v) for s, v in snap.items()
                           if s != eng and v > self.sat[eng].get(s, 0) and v > 0)
            self.cnt[eng] += 1
            self.streams[eng].append((waits, lambda h: h.nop(), eng, 1))
            for sv, vv in waits:
                self.sat[eng][sv] = max(self.sat[eng].get(sv, 0), vv)

    def dma(self, dst, src, reads=(), writes=(), q="sp", **kw):
        return self.op(q, lambda h: h.dma_start(out=dst, in_=src, **kw),
                       reads=reads, writes=writes, dma=True)

    def build(self, block):
        semh = self.semh
        finals = dict(self.cnt)

        def mk(name):
            stream = self.streams[name]

            def body(h):
                for waits, fn, seng, inc in stream:
                    for s, v in waits:
                        h.wait_ge(semh[s], v)
                    fn(h).then_inc(semh[seng], inc)
                if name == "sp":
                    for s, v in finals.items():
                        if v > 0:
                            h.wait_ge(semh[s], v)
            return body

        block.tensor(mk("pe"))
        block.scalar(mk("act"))
        block.vector(mk("dve"))
        block.gpsimd(mk("pool"))
        block.sync(mk("sp"))


def tile_lhsT(w_in_out, mo=128):
    d_in, d_out = w_in_out.shape
    n_kc, n_oc = d_in // 128, (d_out + mo - 1) // mo
    w4 = w_in_out.reshape(n_kc, 128, n_oc, mo)
    return np.ascontiguousarray(w4.transpose(2, 1, 0, 3).reshape(n_oc, 128, n_kc * mo))


def stripe(v, p=128):
    return np.ascontiguousarray(np.asarray(v).reshape(-1, p).T)


def build_nc(phases=_PHASES):
    nc = bass.Bass("TRN2", target_bir_lowering=False)
    ctx = ExitStack()

    def din(name, shape):
        return nc.dram_tensor(name, list(shape), F32, kind="ExternalInput").ap()

    xT_d = din("xT", (D, NTOK))
    wq_d = din("l0_wq", (8, 128, 1024))
    wk_d = din("l0_wk", (8, 128, 1024))
    wv_d = din("l0_wv", (8, 128, 1024))
    wo_d = din("l0_wo", (8, 64, 2048))
    bq_d = din("l0_bq", (128, 8))
    bk_d = din("l0_bk", (128, 8))
    bv_d = din("l0_bv", (128, 8))
    bo_d = din("l0_bo", (128, 8))
    w1_d = din("l0_w1", (32, 128, 1024))
    w2_d = din("l0_w2", (32, 128, 1024))
    h1_d = din("l0_h1", (32, 128, 4096))
    h2_d = din("l0_h2", (32, 128, 4096))
    w3_d = din("l0_w3", (8, 128, 4096))
    w1s_d = din("l0_w1s", (128, 32))
    w2s_d = din("l0_w2s", (128, 32))
    h1s_d = din("l0_h1s", (128, 32))
    h2s_d = din("l0_h2s", (128, 32))
    w3s_d = din("l0_w3s", (128, 8))
    qkv_d = din("l1_wqkv", (24, 128, 1024))
    proj_d = din("l1_proj", (128, 64))
    wout_d = din("l1_wout", (8, 128, 1024))
    bout_d = din("l1_bout", (128, 8))
    rw1_d = din("rw1", (32, 128, 1024))
    rb1_d = din("rb1", (128, 32))
    rw2_d = din("rw2", (1, 128, 256))
    rb2_d = din("rb2", (8, 1))
    w1e_d = din("w1e", (8, 32, 128, 1024))
    b1e_d = din("b1e", (8, 128, 32))
    w2e_d = din("w2e", (8, 8, 128, 4096))
    b2e_d = din("b2e", (8, 128, 8))
    yT_d = nc.dram_tensor("yT", [D, NTOK], F32, kind="ExternalOutput").ap()
    dbg_d = nc.dram_tensor("dbg", [128, 65536], BF16, kind="ExternalOutput").ap() \
        if _DBG else None

    ident_bf_d = nc.inline_tensor(np.eye(128, dtype=BFNP), name="ident_bf").ap()
    ident_f_d = nc.inline_tensor(np.eye(128, dtype=np.float32), name="ident_f").ap()
    sel8_np = np.zeros((8, 8 * 128), dtype=BFNP)
    for e in range(8):
        sel8_np[e, e * 128:(e + 1) * 128] = BFNP(1.0)
    sel8_d = nc.inline_tensor(sel8_np, name="sel8").ap()
    ones_d = nc.inline_tensor(np.ones((128, 128), dtype=BFNP), name="ones_c").ap()
    nones_d = nc.inline_tensor(np.full((128, 128), -1.0, dtype=BFNP),
                               name="nones_c").ap()

    A = nc.alloc_sbuf_tensor
    x_res = [A(f"xres{i}", [128, NTOK], BF16) for i in range(8)]
    xn = [A(f"xn{i}", [128, 1024], BF16) for i in range(8)]
    hatQ = A("hatQ", [128, 16384], BF16)
    hatK = A("hatK", [128, 16384], BF16)
    hatV = A("hatV", [128, 16384], BF16)
    GF = [A(f"gf{i}", [128, NTOK], F32) for i in range(2)]
    wf32 = [A(f"wf32_{i}", [128, 1024], F32) for i in range(1)]
    wbf = [A(f"wbf{i}", [128, 4096], BF16) for i in range(3)]
    Pbf = [A(f"pbf{i}", [128, NTOK], BF16) for i in range(2)]
    ones_bf = A("ones_bf", [128, 128], BF16)
    ident_bf = A("identb", [128, 128], BF16)
    ident_f = A("identf", [128, 128], F32)
    biases = A("biases", [128, 48], F32)
    scales = A("scales", [128, 136], F32)
    rb1_sb = A("rb1sb", [128, 32], F32)
    projbf = A("projbf", [128, 64], BF16)
    eb = [A(f"eb{i}", [128, 40], F32) for i in range(2)]
    smT = [A(f"smT{i}", [128, 512], BF16) for i in range(2)]
    pfbf = [A(f"pfbf{i}", [128, 1024], BF16) for i in range(2)]
    cols = A("colsb", [128, 8], F32)
    cbias = A("cbias", [128, 4], F32)

    psA = nc.alloc_psum_tensor("psA", [128, 2048], F32)
    psB = nc.alloc_psum_tensor("psB", [128, 2048], F32)

    em = Em(nc, ctx)

    def RL(n):
        return [Res() for _ in range(n)]

    r_x = [RL(4) for _ in range(8)]
    r_xn = RL(8)
    r_hQ, r_hK, r_hV = RL(32), RL(32), RL(32)
    r_GF = [RL(4) for _ in range(2)]
    r_wf = RL(1)
    r_wbf = RL(3)
    r_wbfB = RL(3)
    r_P = RL(2)
    r_const = Res()
    r_bias = Res()
    r_scale = Res()
    r_rb1 = Res()
    r_proj = Res()
    r_eb = RL(2)
    r_smT = RL(2)
    r_pfbf = RL(2)
    r_cols = Res()
    r_colsP = [Res(), Res()]
    r_cbias = Res()
    r_rep = Res()
    r_wesb = RL(2)
    r_psA = RL(4)
    r_psB = RL(4)

    def slots(rlist, c0, c1):
        return rlist[c0 // 512: (c1 + 511) // 512]

    def psres(group, c0, c1):
        rl = r_psA if group is psA else r_psB
        return rl[c0 // 512: (c1 + 511) // 512]

    def x_slots(f, c0, c1):
        return r_x[f][c0 // 512: (c1 + 511) // 512]

    em.dma(ones_bf[:], ones_d, writes=[r_const])
    em.op("dve", lambda h: h.memset(cbias[:, 0:1], 1e-5), writes=[r_cbias])
    em.op("dve", lambda h: h.memset(cbias[:, 1:2], 1e-8), writes=[r_cbias])
    em.op("dve", lambda h: h.memset(cbias[:, 2:3], -1.0), writes=[r_cbias])
    em.op("dve", lambda h: h.memset(cbias[:, 3:4], 1.0), writes=[r_cbias])
    em.dma(ident_bf[:], ident_bf_d, writes=[r_const])
    em.dma(ident_f[:], ident_f_d, writes=[r_const])
    em.dma(biases[:, 0:8], bq_d, writes=[r_bias])
    em.dma(biases[:, 8:16], bk_d, writes=[r_bias])
    em.dma(biases[:, 16:24], bv_d, writes=[r_bias])
    em.dma(biases[:, 24:32], bo_d, writes=[r_bias])
    em.dma(biases[:, 32:40], bout_d, writes=[r_bias])
    em.dma(biases[0:8, 40:41], rb2_d, writes=[r_bias])
    em.op("dve", lambda h: h.tensor_scalar(out=biases[:, 0:16], in0=biases[:, 0:16],
                                           scalar1=float(1 / 32.0), scalar2=None,
                                           op0=ALU.mult),
          reads=[r_bias], writes=[r_bias])
    em.dma(scales[:, 0:32], w1s_d, writes=[r_scale])
    em.dma(scales[:, 32:64], w2s_d, writes=[r_scale])
    em.dma(scales[:, 64:96], h1s_d, writes=[r_scale])
    em.dma(scales[:, 96:128], h2s_d, writes=[r_scale])
    em.dma(scales[:, 128:136], w3s_d, writes=[r_scale])
    em.dma(rb1_sb[:], rb1_d, writes=[r_rb1])

    for f in range(8):
        g = f % 2
        em.dma(GF[g][:, :], xT_d[128 * f:128 * (f + 1), :], writes=r_GF[g],
               q=("sp", "pool")[g])
        em.op("pool", lambda h, d=x_res[f], g=g: h.tensor_copy(d[:], GF[g][:, :]),
              reads=r_GF[g], writes=r_x[f])

    wcache = {}
    wc_res = {}
    wc_loaded = set()
    _castrr = [0]

    _qrr = [0]

    def wq():
        _qrr[0] += 1
        return ("sp", "pool")[_qrr[0] % 2]

    def cast_op(dst, src, reads, writes):
        eng = _CASTENG or "act"
        _castrr[0] += 1
        if eng == "act":
            em.op("act", lambda h, d=dst, s=src: h.activation(
                out=d, in_=s, func=AF.Identity), reads=reads, writes=writes)
        else:
            em.op(eng, lambda h, d=dst, s=src: h.tensor_copy(d, s),
                  reads=reads, writes=writes)

    def wc_row(name, row, ncols, nrows):
        if name not in wcache:
            wcache[name] = nc.dram_tensor(f"wc_{name}", [nrows, 128, ncols],
                                          BF16, kind="Internal").ap()
        if (name, row) not in wc_res:
            wc_res[(name, row)] = [Res() for _ in range((ncols + 1023) // 1024)]
        return wcache[name][row], wc_res[(name, row)]

    def load_lhsT(dram_oc_ap, ncols, slot, ckey=None, off=0):
        if "nocache" in _FFNCUT:
            ckey = None
        wres = ([r_wbf[slot]] if off < 2048 else []) + \
               ([r_wbfB[slot]] if off + ncols > 2048 else [])
        if ckey is not None:
            name, row, nrows = ckey
            cap, cres = wc_row(name, row, ncols, nrows)
        ql = wq()
        if ckey is not None and (ckey[0], ckey[1]) in wc_loaded:
            name, row, nrows = ckey
            cap, cres = wc_row(name, row, ncols, nrows)
            em.dma(wbf[slot][:, off:off + ncols], cap,
                   reads=cres, writes=wres, q=ql)
            return
        if ckey is not None:
            pf_tasks[:] = [t for t in pf_tasks if t[1][:2] != (ckey[0], ckey[1])]
        for c0 in range(0, ncols, 1024):
            w = min(1024, ncols - c0)
            fs = 0
            em.dma(wf32[fs][:, 0:w], dram_oc_ap[:, c0:c0 + w], writes=[r_wf[fs]],
                   q=ql)
            cast_op(wbf[slot][:, off + c0:off + c0 + w], wf32[fs][:, 0:w],
                    [r_wf[fs]], wres)
        if ckey is not None:
            name, row, nrows = ckey
            cap, cres = wc_row(name, row, ncols, nrows)
            em.dma(cap, wbf[slot][:, off:off + ncols],
                   reads=wres, writes=cres, q=ql)
            wc_loaded.add((name, row))

    pf_tasks = []

    def _pf_build():
        rows = []
        for oc in range(32):
            rows.append((w1_d[oc], 1024, ("w1", oc, 32)))
            rows.append((w2_d[oc], 1024, ("w2", oc, 32)))
        for oc in range(32):
            rows.append((h1_d[oc], 4096, ("h1", oc, 32)))
        for oc in range(32):
            rows.append((h2_d[oc], 4096, ("h2", oc, 32)))
        for oc in range(8):
            rows.append((w3_d[oc], 4096, ("w3", oc, 8)))
        for ap, ncols, ckey in rows:
            name, row, nrows = ckey
            nch = ncols // 1024
            for c in range(nch):
                pf_tasks.append((ap[:, 1024 * c:1024 * (c + 1)], ckey, c,
                                 ncols, c == nch - 1))

    _pfrr = [0]

    def prefetch_chunks(n):
        while n > 0 and pf_tasks:
            srcap, ckey, c, ncols, last = pf_tasks.pop(0)
            name, row, nrows = ckey
            if (name, row) in wc_loaded:
                continue
            cap, cres = wc_row(name, row, ncols, nrows)
            s = _pfrr[0] % 2
            _pfrr[0] += 1
            ql = wq()
            g1 = [r_GF[1][2 * s], r_GF[1][2 * s + 1]]
            em.dma(GF[1][:, 1024 * s:1024 * (s + 1)], srcap,
                   writes=g1, q=ql)
            em.op("dve", lambda h, s=s: h.tensor_copy(
                pfbf[s][:, :], GF[1][:, 1024 * s:1024 * (s + 1)]),
                reads=g1, writes=[r_pfbf[s]])
            em.dma(cap[:, 1024 * c:1024 * (c + 1)], pfbf[s][:, :],
                   reads=[r_pfbf[s]], writes=[cres[c]], q=ql)
            if last:
                wc_loaded.add((name, row))
            n -= 1

    if "l0ffn" in phases:
        _pf_build()

    def emit_ln(gc0, gc1, dst=None):
        mdst = GF[0].ap()[:, 0:512]
        rdst = GF[0].ap()[:, 512:1024]
        tmp = GF[0].ap()[:, 1024:1536]
        atmp = GF[0].ap()[:, 1536:2048]
        mres, rres, tres, ares = [r_GF[0][0]], [r_GF[0][1]], [r_GF[0][2]], [r_GF[0][3]]
        c = gc0
        while c < gc1:
            w = min(512, gc1 - c)
            s1 = psB[:, 0:w]
            s2 = psB[:, 512:512 + w]
            for f in range(8):
                sres = x_slots(f, c, c + w)
                em.op("act", lambda h, d=smT[f % 2][:, 0:w],
                      s=x_res[f].ap()[:, c:c + w]:
                      h.activation(out=d, in_=s, func=AF.Square),
                      reads=sres, writes=[r_smT[f % 2]])
                em.op("pe", lambda h, o=s1, r_=x_res[f].ap()[:, c:c + w],
                      st=(f == 0), sp=(f == 7):
                      h.matmul(o, ones_bf[:], r_, start=st, stop=sp),
                      reads=[r_const] + sres, writes=psres(psB, 0, w))
                em.op("pe", lambda h, o=s2, r_=smT[f % 2][:, 0:w],
                      st=(f == 0), sp=(f == 7):
                      h.matmul(o, ones_bf[:], r_, start=st, stop=sp),
                      reads=[r_const, r_smT[f % 2]], writes=psres(psB, 512, 512 + w))
            em.op("dve", lambda h, w=w, s1=s1: h.tensor_scalar(
                out=mdst[:, 0:w], in0=s1, scalar1=float(1.0 / D),
                scalar2=None, op0=ALU.mult),
                reads=psres(psB, 0, w), writes=mres)
            em.op("dve", lambda h, w=w, s2=s2: h.tensor_scalar(
                out=tmp[:, 0:w], in0=s2, scalar1=float(1.0 / D),
                scalar2=None, op0=ALU.mult),
                reads=psres(psB, 512, 512 + w), writes=tres)
            em.op("dve", lambda h, w=w: h.tensor_tensor(
                out=rdst[:, 0:w], in0=mdst[:, 0:w], in1=mdst[:, 0:w], op=ALU.mult),
                reads=mres, writes=rres)
            em.op("dve", lambda h, w=w: h.tensor_tensor(
                out=tmp[:, 0:w], in0=tmp[:, 0:w], in1=rdst[:, 0:w], op=ALU.subtract),
                reads=tres + rres, writes=tres)
            em.op("act", lambda h, w=w: h.activation(
                out=tmp[:, 0:w], in_=tmp[:, 0:w], func=AF.Ln, bias=cbias[:, 0:1]),
                reads=tres + [r_cbias], writes=tres)
            em.op("act", lambda h, w=w: h.activation(
                out=rdst[:, 0:w], in_=tmp[:, 0:w], func=AF.Exp, scale=float(-0.5)),
                reads=tres, writes=rres)
            for f in range(8):
                sres = x_slots(f, c, c + w)
                em.op("dve", lambda h, w=w, f=f, c=c: h.tensor_tensor(
                    out=atmp[:, 0:w], in0=x_res[f].ap()[:, c:c + w], in1=mdst[:, 0:w],
                    op=ALU.subtract),
                    reads=sres + mres, writes=ares)
                if dst is None:
                    oap, ores = xn[f].ap()[:, c - gc0:c - gc0 + w], [r_xn[f]]
                else:
                    oap, ores = dst(f, c, w)
                em.op("dve", lambda h, w=w, oap=oap: h.tensor_tensor(
                    out=oap, in0=atmp[:, 0:w], in1=rdst[:, 0:w],
                    op=ALU.mult),
                    reads=ares + rres, writes=ores)
            if "ln" in _BAR:
                em.barrier()
            c += w

    def proj(dram, n_oc, n_kc, tok_chunks, rhs_fn, evac,
             pbanks=((psB, 0), (psB, 512), (psB, 1024), (psB, 1536)), ckey=None):
        ncols = n_kc * 128
        for oc in range(n_oc):
            slot = oc % 3
            load_lhsT(dram[oc], ncols, slot,
                      None if ckey is None else (ckey[0], ckey[1] + oc, ckey[2]))
            for ci, (c0, w) in enumerate(tok_chunks):
                grp, base = pbanks[(oc * len(tok_chunks) + ci) % len(pbanks)]
                ps = grp[:, base:base + w]
                pres = psres(grp, base, base + w)
                for kc in range(n_kc):
                    rhs, rreads = rhs_fn(kc, c0, w)
                    em.op("pe", lambda h, o=ps, l=wbf[slot][:, 128 * kc:128 * (kc + 1)],
                          r_=rhs, st=(kc == 0), sp=(kc == n_kc - 1):
                          h.matmul(o, l, r_, start=st, stop=sp),
                          reads=[r_wbf[slot]] + rreads, writes=pres)
                evac(oc, ci, c0, w, ps, pres)
            prefetch_chunks(1)

    def xn_rhs_at(base):
        def f(kc, c0, w):
            return xn[kc].ap()[:, c0 - base:c0 - base + w], [r_xn[kc]]
        return f

    def phase_l0attn():
        for H in range(2):
            t0 = 1024 * H
            emit_ln(t0, t0 + 1024)
            tok_chunks = [(t0, 512), (t0 + 512, 512)]

            def mk_evac(hat, hres, bcol, scl):
                def cb(oc, ci, c0, w, ps, pres):
                    c4 = (c0 - t0) // 128
                    psv = ps.rearrange("p (jh jl) -> p jh jl", jl=128)
                    hat4 = hat.ap().rearrange("p (el oo t) -> p el oo t", oo=8, t=128)
                    dst = hat4[:, c4:c4 + 4, oc, :]
                    sl = slots(hres, 1024 * c4, 1024 * (c4 + 4))
                    em.op("act", lambda h, d=dst, s=psv, b=biases[:, bcol + oc:bcol + oc + 1],
                          sc=float(scl):
                          h.activation(out=d, in_=s, func=AF.Identity, bias=b, scale=sc),
                          reads=pres + [r_bias], writes=sl)
                return cb

            if _L0CUT == "ln":
                continue
            proj(wq_d, 8, 8, tok_chunks, xn_rhs_at(t0), mk_evac(hatQ, r_hQ, 0, 1 / 32.0), ckey=("wq", 0, 8))
            proj(wk_d, 8, 8, tok_chunks, xn_rhs_at(t0), mk_evac(hatK, r_hK, 8, 1 / 32.0), ckey=("wk", 0, 8))
            proj(wv_d, 8, 8, tok_chunks, xn_rhs_at(t0), mk_evac(hatV, r_hV, 16, 1.0), ckey=("wv", 0, 8))

            em.dma(hatV.ap()[0:64, 8192:16384], hatK.ap()[64:128, 0:8192],
                   reads=slots(r_hK, 0, 8192), writes=slots(r_hV, 8192, 16384),
                   q="pool")
            em.dma(hatV.ap()[64:128, 8192:16384], hatK.ap()[0:64, 0:8192],
                   reads=slots(r_hK, 0, 8192), writes=slots(r_hV, 8192, 16384),
                   q="pool")

            if _L0CUT == "qkv":
                continue
            for el in range(8):
                base = 1024 * el
                qsl = slots(r_hQ, base, base + 1024)
                ksl = slots(r_hK, base, base + 1024)
                vsl = slots(r_hV, base, base + 1024)
                def emit_accum(gb):
                    for j in range(4):
                        em.op("pe", lambda h, o=psB[:, 512 * j:512 * (j + 1)],
                              r_=Pbf[gb % 2].ap()[:, 512 * j:512 * (j + 1)],
                              st=(gb == 0), sp=(gb == 15):
                              h.matmul(o, ones_bf[:], r_, start=st, stop=sp),
                              reads=[r_const, r_P[gb % 2]], writes=[r_psB[j]])

                for gb in range(16):
                    shb, ocb = gb % 2, gb // 2
                    for j in range(4):
                        shj, cj = j // 2, j % 2
                        if shj == shb:
                            lt = hatK.ap()[64 * shj:64 * (shj + 1),
                                           base + 128 * ocb:base + 128 * (ocb + 1)]
                            lres = ksl
                        else:
                            lt = hatV.ap()[64 * shj:64 * (shj + 1),
                                           8192 + base + 128 * ocb:
                                           8192 + base + 128 * (ocb + 1)]
                            lres = slots(r_hV, 8192 + base, 8192 + base + 1024)
                        rhs = hatQ.ap()[64 * shj:64 * (shj + 1),
                                        base + 512 * cj:base + 512 * (cj + 1)]
                        em.op("pe", lambda h, o=psA[:, 512 * j:512 * (j + 1)], l=lt,
                              r_=rhs: h.matmul(o, l, r_, start=True, stop=True),
                              reads=lres + qsl, writes=[r_psA[j]])
                    if gb > 0:
                        emit_accum(gb - 1)
                    cp = 4 * (gb % 2)
                    rcp = r_colsP[gb % 2]
                    for hh in range(2):
                        em.op("act", lambda h, hh=hh, cp=cp, d=Pbf[gb % 2]:
                              h.activation(
                                  out=d[:, 1024 * hh:1024 * (hh + 1)],
                                  in_=psA[:, 1024 * hh:1024 * (hh + 1)], func=AF.Exp,
                                  bias=cbias[:, 2:3],
                                  accum_out=cols[:, cp + hh:cp + hh + 1]),
                              reads=[r_psA[2 * hh], r_psA[2 * hh + 1], r_cbias, rcp],
                              writes=[r_P[gb % 2], rcp])
                    em.op("dve", lambda h, cp=cp: h.tensor_tensor(
                        out=cols[:, cp + 2:cp + 3], in0=cols[:, cp:cp + 1],
                        in1=cols[:, cp + 1:cp + 2], op=ALU.add),
                        reads=[rcp], writes=[rcp])
                    em.op("dve", lambda h, cp=cp: h.reciprocal(
                        out=cols[:, cp + 3:cp + 4], in_=cols[:, cp + 2:cp + 3]),
                        reads=[rcp], writes=[rcp])
                    em.op("dve", lambda h, cp=cp, d=Pbf[gb % 2]: h.tensor_scalar(
                        out=d[:], in0=d[:], scalar1=cols[:, cp + 3:cp + 4],
                        scalar2=None, op0=ALU.mult),
                        reads=[r_P[gb % 2], rcp], writes=[r_P[gb % 2]])
                    if gb % 2 == 0:
                        prefetch_chunks(1)
                emit_accum(15)
                for sh in range(2):
                    em.op("dve", lambda h, sh=sh,
                          d=hatQ.ap()[64 * sh:64 * (sh + 1), base:base + 1024],
                          v=hatV.ap()[64 * sh:64 * (sh + 1), base:base + 1024],
                          r_=psB[64 * sh:64 * (sh + 1), 1024 * sh:1024 * (sh + 1)]:
                          h.tensor_tensor(out=d, in0=v, in1=r_, op=ALU.mult),
                          reads=vsl + r_psB, writes=qsl)

            if _L0CUT == "attn":
                continue
            em.dma(hatQ.ap()[0:64, 8192:16384], hatQ.ap()[64:128, 0:8192],
                   reads=slots(r_hQ, 0, 8192), writes=slots(r_hQ, 8192, 16384),
                   q="pool")
            hat4Q = hatQ.ap().rearrange("p (el oo t) -> p el oo t", oo=8, t=128)
            hat4Qd = hatQ.ap()[:, 8192:16384].rearrange(
                "p (el oo t) -> p el oo t", oo=8, t=128)
            for oc in range(8):
                slot = oc % 3
                ql = wq()
                if ("wo", oc) in wc_loaded:
                    cap, cres = wc_row("wo", oc, 2048, 8)
                    em.dma(wbf[slot][0:64, 0:2048], cap[0:64, :],
                           reads=cres, writes=[r_wbf[slot]], q=ql)
                else:
                    for pc in range(2):
                        fs = 0
                        em.dma(wf32[fs][0:64, 0:1024],
                               wo_d[oc, :, 1024 * pc:1024 * (pc + 1)],
                               writes=[r_wf[fs]], q=ql)
                        cast_op(wbf[slot][0:64, 1024 * pc:1024 * (pc + 1)],
                                wf32[fs][0:64, 0:1024],
                                [r_wf[fs]], [r_wbf[slot]])
                    cap, cres = wc_row("wo", oc, 2048, 8)
                    em.dma(cap[0:64, :], wbf[slot][0:64, 0:2048],
                           reads=[r_wbf[slot]], writes=cres, q=ql)
                    wc_loaded.add(("wo", oc))
                for ci in range(2):
                    bk = 512 * ((2 * oc + ci) % 4)
                    ps = psB[:, bk:bk + 512]
                    pres = psres(psB, bk, bk + 512)
                    el0 = 4 * ci
                    for kc in range(16):
                        lt = wbf[slot][0:64, 128 * kc:128 * (kc + 1)]
                        hv = hat4Q if kc % 2 == 0 else hat4Qd
                        rhs = hv[0:64, el0:el0 + 4, kc // 2, :]
                        rs = (slots(r_hQ, 1024 * el0, 1024 * (el0 + 4))
                              if kc % 2 == 0 else
                              slots(r_hQ, 8192 + 1024 * el0, 8192 + 1024 * (el0 + 4)))
                        em.op("pe", lambda h, o=ps, l=lt, r_=rhs, st=(kc == 0),
                              sp=(kc == 15): h.matmul(o, l, r_, start=st, stop=sp),
                              reads=[r_wbf[slot]] + rs, writes=pres)
                    gt = t0 + 512 * ci
                    if _L0CUT == "womm":
                        em.op("dve", lambda h, p=ps: h.tensor_copy(
                            GF[1][:, 0:512], p), reads=pres, writes=[r_GF[1][0]])
                    else:
                        em.op("dve", lambda h, o=x_res[oc].ap()[:, gt:gt + 512], p=ps,
                              b=biases[:, 24 + oc:25 + oc]:
                              h.scalar_tensor_tensor(out=o, in0=p, scalar=b, in1=o,
                                                     op0=ALU.add, op1=ALU.add),
                              reads=pres + [r_bias] + x_slots(oc, gt, gt + 512),
                              writes=x_slots(oc, gt, gt + 512))

    def phase_l0ffn():
        for tq in range(4):
            t0 = 512 * tq
            if "noln" not in _FFNCUT:
                emit_ln(t0, t0 + 512)

            for oc in range(32):
                slot = oc % 3
                load_lhsT(w1_d[oc], 1024, slot, ckey=("w1", oc, 32))
                bk1 = 512 * ((2 * oc) % 4)
                ps1 = psB[:, bk1:bk1 + 512]
                for kc in range(8):
                    if "noln" in _FFNCUT:
                        r_ap, r_rd = x_res[kc].ap()[:, t0:t0 + 512], x_slots(kc, t0, t0 + 512)
                    else:
                        r_ap, r_rd = xn[kc].ap()[:, 0:512], [r_xn[kc]]
                    em.op("pe", lambda h, o=ps1,
                          l=wbf[slot][:, 128 * kc:128 * (kc + 1)],
                          r_=r_ap, st=(kc == 0), sp=(kc == 7):
                          h.matmul(o, l, r_, start=st, stop=sp),
                          reads=[r_wbf[slot]] + r_rd, writes=psres(psB, bk1, bk1 + 512))
                if oc % 2 == 0:
                    tA, tB = smT[0][:, 0:512], smT[1][:, 0:512]
                    rA, rB = r_smT[0], r_smT[1]
                else:
                    tA, tB = Pbf[0].ap()[:, 0:512], Pbf[0].ap()[:, 512:1024]
                    rA, rB = r_P[0], r_P[0]
                em.op("act", lambda h, oc=oc, ps1=ps1, tA=tA: h.activation(
                    out=tA, in_=ps1, func=AF.Silu,
                    scale=scales[:, oc:oc + 1]),
                    reads=psres(psB, bk1, bk1 + 512) + [r_scale], writes=[rA])
                load_lhsT(w2_d[oc], 1024, slot, ckey=("w2", oc, 32), off=2048)
                bk2 = 512 * ((2 * oc + 1) % 4)
                ps2 = psB[:, bk2:bk2 + 512]
                for kc in range(8):
                    if "noln" in _FFNCUT:
                        r_ap, r_rd = x_res[kc].ap()[:, t0:t0 + 512], x_slots(kc, t0, t0 + 512)
                    else:
                        r_ap, r_rd = xn[kc].ap()[:, 0:512], [r_xn[kc]]
                    em.op("pe", lambda h, o=ps2,
                          l=wbf[slot][:, 2048 + 128 * kc:2048 + 128 * (kc + 1)],
                          r_=r_ap, st=(kc == 0), sp=(kc == 7):
                          h.matmul(o, l, r_, start=st, stop=sp),
                          reads=[r_wbfB[slot]] + r_rd, writes=psres(psB, bk2, bk2 + 512))
                em.op("dve", lambda h, oc=oc, ps2=ps2, tB=tB: h.tensor_scalar(
                    out=tB, in0=ps2, scalar1=scales[:, 32 + oc:33 + oc],
                    scalar2=None, op0=ALU.mult),
                    reads=psres(psB, bk2, bk2 + 512) + [r_scale], writes=[rB])
                em.op("dve", lambda h, d=hatQ.ap()[:, 512 * oc:512 * (oc + 1)],
                      tA=tA, tB=tB:
                      h.tensor_tensor(out=d, in0=tA, in1=tB, op=ALU.mult),
                      reads=[rA, rB], writes=[r_hQ[oc]])
                prefetch_chunks(1)
                if "ffn" in _BAR:
                    em.barrier()

            def hstage(dram, src_hat, src_res, dst_cb, n_oc, cname):
                for oc in range(n_oc):
                    slot = oc % 3
                    load_lhsT(dram[oc], 4096, slot, ckey=(cname, oc, n_oc))
                    bk = 512 * (oc % 4)
                    ps = psB[:, bk:bk + 512]
                    pres = psres(psB, bk, bk + 512)
                    for kc in range(32):
                        em.op("pe", lambda h, o=ps,
                              l=wbf[slot][:, 128 * kc:128 * (kc + 1)],
                              r_=src_hat.ap()[:, 512 * kc:512 * (kc + 1)],
                              st=(kc == 0), sp=(kc == 31):
                              h.matmul(o, l, r_, start=st, stop=sp),
                              reads=[r_wbf[slot], r_wbfB[slot], src_res[kc]],
                              writes=pres)
                    dst_cb(oc, ps, pres)
                    prefetch_chunks(3)
                    if "ffn" in _BAR:
                        em.barrier()

            def gelu_to(dst_hat, dst_res, scol):
                def cb(oc, ps, pres):
                    em.op("act", lambda h, oc=oc, ps=ps,
                          d=dst_hat.ap()[:, 512 * oc:512 * (oc + 1)]:
                          h.activation(out=d, in_=ps, func=AF.Gelu,
                                       scale=scales[:, scol + oc:scol + oc + 1]),
                          reads=pres + [r_scale], writes=[dst_res[oc]])
                return cb

            if _DBG:
                em.dma(dbg_d[:, 16384 * tq:16384 * (tq + 1)], hatQ.ap()[:, :],
                       reads=list(r_hQ))
            if _FFNCUT != "w12":
                hstage(h1_d, hatQ, r_hQ, gelu_to(hatK, r_hK, 64), 32, "h1")
                if _FFNCUT != "h1":
                    hstage(h2_d, hatK, r_hK, gelu_to(hatQ, r_hQ, 96), 32, "h2")

            def w3_cb(oc, ps, pres):
                em.op("dve", lambda h, oc=oc, ps=ps,
                      o=x_res[oc].ap()[:, t0:t0 + 512]:
                      h.scalar_tensor_tensor(out=o, in0=ps,
                                             scalar=scales[:, 128 + oc:129 + oc],
                                             in1=o, op0=ALU.mult, op1=ALU.add),
                      reads=pres + [r_scale] + x_slots(oc, t0, t0 + 512),
                      writes=x_slots(oc, t0, t0 + 512))
            hstage(w3_d, hatQ, r_hQ, w3_cb, 8, "w3")

    def phase_l1attn():
        assert not pf_tasks
        em.dma(GF[0][:, 0:64], proj_d, writes=r_GF[0])
        em.op("act", lambda h: h.activation(out=projbf[:], in_=GF[0][:, 0:64],
                                            func=AF.Identity),
              reads=r_GF[0], writes=[r_proj])

        def mk_evac(hat, hres):
            def cb(oc, ci, c0, w, ps, pres):
                d = hat.ap()[:, 2048 * oc + c0:2048 * oc + c0 + w]
                sl = slots(hres, 2048 * oc + c0, 2048 * oc + c0 + w)
                em.op("act", lambda h, d=d, ps=ps:
                      h.activation(out=d, in_=ps, func=AF.Identity),
                      reads=pres, writes=sl)
            return cb

        def phi(dst_off):
            for p in range(4):
                for sh in range(2):
                    hd = 2 * p + sh
                    src = hatQ.ap()[:, 2048 * hd:2048 * (hd + 1)]
                    sres = slots(r_hQ, 2048 * hd, 2048 * (hd + 1))
                    for c0 in range(0, 2048, 512):
                        em.op("pe", lambda h, o=psA[64 * sh:64 * (sh + 1), c0:c0 + 512],
                              r_=src[:, c0:c0 + 512], tp=(0, 64 * sh):
                              h.matmul(o, projbf[:, 0:64], r_, start=True, stop=True,
                                       tile_position=tp),
                              reads=[r_proj] + sres, writes=psres(psA, c0, c0 + 512))
                em.op("act", lambda h: h.activation(
                    out=GF[0][:, :], in_=psA[:], func=AF.Relu),
                    reads=r_psA, writes=r_GF[0])
                em.op("act", lambda h: h.activation(
                    out=GF[1][:, :], in_=psA[:], func=AF.Relu, scale=float(-1.0)),
                    reads=r_psA, writes=r_GF[1])
                em.op("act", lambda h: h.activation(
                    out=GF[1][:, :], in_=GF[1][:, :], func=AF.Exp,
                    scale=float(-1.0)),
                    reads=r_GF[1], writes=r_GF[1])
                dst = hatV.ap()[:, dst_off + 2048 * p:dst_off + 2048 * (p + 1)]
                dsl = slots(r_hV, dst_off + 2048 * p, dst_off + 2048 * (p + 1))
                em.op("dve", lambda h, d=dst: h.tensor_tensor(
                    out=d, in0=GF[0][:, :], in1=GF[1][:, :], op=ALU.add),
                    reads=r_GF[0] + r_GF[1], writes=dsl)

        def qkv_half(sec_d, hat, hres, row0):
            for H in range(2):
                t0 = 1024 * H
                emit_ln(t0, t0 + 1024)
                tok_chunks = [(t0, 512), (t0 + 512, 512)]
                proj(sec_d, 8, 8, tok_chunks, xn_rhs_at(t0), mk_evac(hat, hres),
                     ckey=("qkv", row0, 24))

        qkv_half(qkv_d, hatQ, r_hQ, 0)
        phi(0)
        qkv_half(qkv_d[8:16], hatQ, r_hQ, 8)
        phi(8192)
        qkv_half(qkv_d[16:24], hatK, r_hK, 16)

        for hd in range(8):
            p, sh = hd // 2, hd % 2
            pq = hatV.ap()[64 * sh:64 * (sh + 1), 2048 * p:2048 * (p + 1)]
            pk = hatV.ap()[64 * sh:64 * (sh + 1), 8192 + 2048 * p:8192 + 2048 * (p + 1)]
            pqres = slots(r_hV, 2048 * p, 2048 * (p + 1))
            pkres = slots(r_hV, 8192 + 2048 * p, 8192 + 2048 * (p + 1))
            vt = hatK.ap()[:, 2048 * hd:2048 * (hd + 1)]
            vres = slots(r_hK, 2048 * hd, 2048 * (hd + 1))
            em.op("pool", lambda h: h.memset(GF[0][:, :], 0.0), writes=r_GF[0])
            for di, dl in enumerate(range(-3, 4)):
                a = max(0, -dl)
                b = min(2048, 2048 - dl)
                prod = Pbf[0].ap()[0:64, :]
                em.op("dve", lambda h, a=a, b=b, dl=dl, prod=prod, pq=pq, pk=pk:
                      h.tensor_tensor(out=prod[:, a:b], in0=pq[:, a:b],
                                      in1=pk[:, a + dl:b + dl], op=ALU.mult),
                      reads=pqres + pkres, writes=[r_P[0]])
                for bank in range(4):
                    s0, s1 = max(a, 512 * bank), min(b, 512 * (bank + 1))
                    if s0 >= s1:
                        continue
                    em.op("pe", lambda h, s0=s0, s1=s1, prod=prod: h.matmul(
                        psA[:, s0:s1], ones_bf[0:64, :], prod[:, s0:s1],
                        start=True, stop=True),
                        reads=[r_const, r_P[0]], writes=[r_psA[bank]])
                    em.op("pe", lambda h, s0=s0, s1=s1, prod=prod, st=(di == 0),
                          sp=(di == 6):
                          h.matmul(psB[:, s0:s1], ones_bf[0:64, :], prod[:, s0:s1],
                                   start=st, stop=sp),
                          reads=[r_const, r_P[0]], writes=[r_psB[bank]])
                em.op("dve", lambda h, a=a, b=b, dl=dl, vt=vt:
                      h.scalar_tensor_tensor(out=Pbf[1].ap()[:, a:b],
                                             in0=vt[:, a + dl:b + dl],
                                             scalar=float(1.0),
                                             in1=psA[:, a:b],
                                             op0=ALU.mult, op1=ALU.mult),
                      reads=vres + psres(psA, a, b), writes=[r_P[1]])
                em.op("pool", lambda h, a=a, b=b: h.tensor_tensor(
                    out=GF[0][:, a:b], in0=GF[0][:, a:b], in1=Pbf[1].ap()[:, a:b],
                    op=ALU.add),
                    reads=[r_P[1]] + r_GF[0], writes=r_GF[0])
            em.op("act", lambda h: h.activation(
                out=GF[1][:, :], in_=psB[:], func=AF.Ln, bias=cbias[:, 1:2]),
                reads=r_psB + [r_cbias], writes=r_GF[1])
            em.op("act", lambda h: h.activation(
                out=GF[1][:, :], in_=GF[1][:, :], func=AF.Exp, scale=float(-1.0)),
                reads=r_GF[1], writes=r_GF[1])
            osl = slots(r_hQ, 2048 * hd, 2048 * (hd + 1))
            em.op("dve", lambda h, hd=hd: h.tensor_tensor(
                out=hatQ.ap()[:, 2048 * hd:2048 * (hd + 1)],
                in0=GF[0][:, :], in1=GF[1][:, :], op=ALU.mult),
                reads=r_GF[0] + r_GF[1], writes=osl)

        def out_rhs(kc, c0, w):
            return (hatQ.ap()[:, 2048 * kc + c0:2048 * kc + c0 + w],
                    slots(r_hQ, 2048 * kc + c0, 2048 * kc + c0 + w))

        def evac_out(oc, ci, c0, w, ps, pres):
            em.op("dve", lambda h, o=x_res[oc].ap()[:, c0:c0 + w], ps=ps,
                  b=biases[:, 32 + oc:33 + oc]:
                  h.scalar_tensor_tensor(out=o, in0=ps, scalar=b, in1=o,
                                         op0=ALU.add, op1=ALU.add),
                  reads=pres + [r_bias] + x_slots(oc, c0, c0 + w),
                  writes=x_slots(oc, c0, c0 + w))
        proj(wout_d, 8, 8, [(0, 512), (512, 512), (1024, 512), (1536, 512)],
             out_rhs, evac_out)

    def phase_moe():
        lgT = GF[1].ap()[0:8, 0:2048]
        em.dma(Pbf[1].ap()[0:8, 0:1024], sel8_d, writes=[r_P[1]])
        em.dma(wf32[0][:, 0:256], rw2_d[0, :, :], writes=[r_wf[0]])
        em.op("pool", lambda h: h.tensor_copy(wbf[0][:, 3840:4096], wf32[0][:, 0:256]),
              reads=[r_wf[0]], writes=[r_wbfB[0]])
        for tc in range(4):
            t0 = 512 * tc
            emit_ln(t0, t0 + 512)
            for oc in range(32):
                slot = oc % 3
                load_lhsT(rw1_d[oc], 1024, slot, ckey=("rw1", oc, 32))
                bk = 512 * (oc % 4)
                ps = psB[:, bk:bk + 512]
                pres = psres(psB, bk, bk + 512)
                for kc in range(8):
                    em.op("pe", lambda h, o=ps,
                          l=wbf[slot][:, 128 * kc:128 * (kc + 1)],
                          r_=xn[kc].ap()[:, 0:512], st=(kc == 0), sp=(kc == 7):
                          h.matmul(o, l, r_, start=st, stop=sp),
                          reads=[r_wbf[slot], r_xn[kc]], writes=pres)
                em.op("act", lambda h, oc=oc, ps=ps,
                      d=hatV.ap()[:, 512 * oc:512 * (oc + 1)]:
                      h.activation(out=d, in_=ps, func=AF.Silu,
                                   bias=rb1_sb[:, oc:oc + 1]),
                      reads=pres + [r_rb1], writes=[r_hV[oc]])
            ps = psA[0:8, 0:512]
            for kc in range(32):
                em.op("pe", lambda h, ps=ps, l=wbf[0][:, 3840 + 8 * kc:3848 + 8 * kc],
                      r_=hatV.ap()[:, 512 * kc:512 * (kc + 1)],
                      st=(kc == 0), sp=(kc == 31):
                      h.matmul(ps, l, r_, start=st, stop=sp),
                      reads=[r_wbfB[0], r_hV[kc]], writes=[r_psA[0]])
            em.op("act", lambda h, tc=tc, ps=ps: h.activation(
                out=lgT[:, 512 * tc:512 * (tc + 1)], in_=ps, func=AF.Identity,
                bias=biases[0:8, 40:41]),
                reads=[r_psA[0], r_bias], writes=[r_GF[1][tc]])

        weT = Pbf[0].ap()[0:8, 0:2048]
        r_weT = r_P[0]
        m1 = GF[0].ap()[:, 1280:1296]
        m2 = GF[0].ap()[:, 1296:1312]
        dcol = GF[0].ap()[:, 1312:1328]
        p1 = GF[0].ap()[:, 1328:1344]
        r_m = [r_GF[0][3]]
        eqs = GF[0]
        for ch in range(16):
            em.op("pe", lambda h, ch=ch: h.transpose(
                psA[:, 512:520], lgT[:, 128 * ch:128 * (ch + 1)], ident_f[0:8, 0:8]),
                reads=r_GF[1] + [r_const], writes=[r_psA[1]])
            em.op("dve", lambda h, ch=ch: h.tensor_copy(
                GF[0].ap()[:, 1024 + 16 * ch:1024 + 16 * ch + 8], psA[:, 512:520]),
                reads=[r_psA[1]], writes=[r_GF[0][2]])
        for ch in range(16):
            lgtm = GF[0].ap()[:, 1024 + 16 * ch:1024 + 16 * ch + 8]
            em.op("dve", lambda h, ch=ch, lgtm=lgtm: h.reduce_max(
                m1[:, ch:ch + 1], lgtm, axis=AX.X),
                reads=[r_GF[0][2]], writes=r_m)
            em.op("dve", lambda h, ch=ch, lgtm=lgtm: h.tensor_scalar(
                out=eqs.ap()[:, 16 * ch:16 * ch + 8], in0=lgtm,
                scalar1=m1[:, ch:ch + 1], scalar2=None, op0=ALU.is_equal),
                reads=[r_GF[0][2]] + r_m, writes=[r_GF[0][0]])
            em.op("dve", lambda h, ch=ch, lgtm=lgtm: h.scalar_tensor_tensor(
                out=eqs.ap()[:, 256 + 16 * ch:256 + 16 * ch + 8],
                in0=eqs.ap()[:, 16 * ch:16 * ch + 8], scalar=float(-1e30),
                in1=lgtm, op0=ALU.mult, op1=ALU.add),
                reads=[r_GF[0][0], r_GF[0][2]], writes=[r_GF[0][0]])
            em.op("dve", lambda h, ch=ch: h.reduce_max(
                m2[:, ch:ch + 1], eqs.ap()[:, 256 + 16 * ch:256 + 16 * ch + 8],
                axis=AX.X),
                reads=[r_GF[0][0]], writes=r_m)
            em.op("dve", lambda h, ch=ch: h.tensor_scalar(
                out=eqs.ap()[:, 512 + 16 * ch:512 + 16 * ch + 8],
                in0=eqs.ap()[:, 256 + 16 * ch:256 + 16 * ch + 8],
                scalar1=m2[:, ch:ch + 1], scalar2=None, op0=ALU.is_equal),
                reads=[r_GF[0][0]] + r_m, writes=[r_GF[0][1]])
            em.op("dve", lambda h, ch=ch: h.tensor_tensor(
                out=dcol[:, ch:ch + 1], in0=m1[:, ch:ch + 1], in1=m2[:, ch:ch + 1],
                op=ALU.subtract),
                reads=r_m, writes=r_m)
        em.op("act", lambda h: h.activation(out=p1, in_=dcol, func=AF.Sigmoid),
              reads=r_m, writes=r_m)
        for ch in range(16):
            em.op("dve", lambda h, ch=ch: h.tensor_scalar(
                out=eqs.ap()[:, 768 + 16 * ch:768 + 16 * ch + 8],
                in0=eqs.ap()[:, 512 + 16 * ch:512 + 16 * ch + 8],
                scalar1=p1[:, ch:ch + 1], scalar2=None, op0=ALU.mult),
                reads=[r_GF[0][1]] + r_m, writes=[r_GF[0][1]])
            em.op("dve", lambda h, ch=ch: h.tensor_scalar(
                out=eqs.ap()[:, 16 * ch:16 * ch + 8],
                in0=eqs.ap()[:, 16 * ch:16 * ch + 8],
                scalar1=p1[:, ch:ch + 1], scalar2=None, op0=ALU.mult),
                reads=[r_GF[0][0]] + r_m, writes=[r_GF[0][0]])
            em.op("dve", lambda h, ch=ch: h.tensor_tensor(
                out=eqs.ap()[:, 16 * ch:16 * ch + 8],
                in0=eqs.ap()[:, 16 * ch:16 * ch + 8],
                in1=eqs.ap()[:, 512 + 16 * ch:512 + 16 * ch + 8], op=ALU.add),
                reads=[r_GF[0][0], r_GF[0][1]], writes=[r_GF[0][0]])
            em.op("dve", lambda h, ch=ch: h.tensor_tensor(
                out=eqs.ap()[:, 16 * ch:16 * ch + 8],
                in0=eqs.ap()[:, 16 * ch:16 * ch + 8],
                in1=eqs.ap()[:, 768 + 16 * ch:768 + 16 * ch + 8], op=ALU.subtract),
                reads=[r_GF[0][0], r_GF[0][1]], writes=[r_GF[0][0]])
            em.op("act", lambda h, ch=ch: h.activation(
                out=GF[1].ap()[:, 16 * ch:16 * ch + 8],
                in_=eqs.ap()[:, 16 * ch:16 * ch + 8], func=AF.Identity),
                reads=[r_GF[0][0]], writes=[r_GF[1][0]])
            em.op("pe", lambda h, ch=ch: h.transpose(
                psA[0:8, 640:768], GF[1].ap()[:, 16 * ch:16 * ch + 8], ident_f[:]),
                reads=[r_GF[1][0], r_const], writes=[r_psA[1]])
            em.op("act", lambda h, ch=ch: h.activation(
                out=weT[:, 128 * ch:128 * (ch + 1)], in_=psA[0:8, 640:768],
                func=AF.Identity),
                reads=[r_psA[1]], writes=[r_weT])

        emit_ln(0, 2048,
                dst=lambda f, c, w: (hatV.ap()[:, 2048 * f + c:2048 * f + c + w],
                                     slots(r_hV, 2048 * f + c, 2048 * f + c + w)))
        for e in range(8):
            ebslot = e % 2
            em.dma(eb[ebslot][:, 0:32], b1e_d[e, :, :], writes=[r_eb[ebslot]])
            em.dma(eb[ebslot][:, 32:40], b2e_d[e, :, :], writes=[r_eb[ebslot]])
            for ci in range(4):
                pb = 1024 + 512 * (ci % 2)
                em.op("pe", lambda h, e=e, ci=ci, pb=pb,
                      o=psA[:, pb:pb + 512],
                      l=Pbf[1].ap()[0:8, 128 * e:128 * (e + 1)],
                      r_=weT[:, 512 * ci:512 * (ci + 1)]:
                      h.matmul(o, l, r_, start=True, stop=True),
                      reads=[r_P[1], r_weT],
                      writes=psres(psA, pb, pb + 512))
                em.op("act", lambda h, ci=ci, pb=pb,
                      o=GF[0].ap()[:, 512 * ci:512 * ci + 512],
                      s=psA[:, pb:pb + 512]:
                      h.activation(out=o, in_=s, func=AF.Identity),
                      reads=psres(psA, pb, pb + 512),
                      writes=[r_GF[0][ci]])
            for hp in range(2):
                for oh in range(16):
                    oc = 16 * hp + oh
                    slot = oh % 3
                    load_lhsT(w1e_d[e, oc], 1024, slot)
                    hat = hatQ if oh < 8 else hatK
                    hresl = r_hQ if oh < 8 else r_hK
                    off = 2048 * (oh % 8)
                    for ci in range(4):
                        bk = 512 * (ci % 4)
                        ps = psB[:, bk:bk + 512]
                        pres = psres(psB, bk, bk + 512)
                        for kc in range(8):
                            xno = 2048 * kc + 512 * ci
                            em.op("pe", lambda h, o=ps,
                                  l=wbf[slot][:, 128 * kc:128 * (kc + 1)],
                                  r_=hatV.ap()[:, xno:xno + 512],
                                  st=(kc == 0), sp=(kc == 7):
                                  h.matmul(o, l, r_, start=st, stop=sp),
                                  reads=[r_wbf[slot], r_hV[xno // 512]],
                                  writes=pres)
                        em.op("act", lambda h, oc=oc, ps=ps, ebslot=ebslot,
                              d=hat.ap()[:, off + 512 * ci:off + 512 * (ci + 1)]:
                              h.activation(out=d, in_=ps, func=AF.Silu,
                                           bias=eb[ebslot][:, oc:oc + 1]),
                              reads=pres + [r_eb[ebslot]],
                              writes=[hresl[(off + 512 * ci) // 512]])
                for oc in range(8):
                    slot = oc % 3
                    load_lhsT(w2e_d[e, oc][:, 2048 * hp:2048 * (hp + 1)], 2048, slot)
                    for ci in range(4):
                        bk = 512 * (ci % 4)
                        ps = psB[:, bk:bk + 512]
                        pres = psres(psB, bk, bk + 512)
                        for kc in range(16):
                            hat = hatQ if kc < 8 else hatK
                            hresl = r_hQ if kc < 8 else r_hK
                            off = 2048 * (kc % 8)
                            em.op("pe", lambda h, o=ps,
                                  l=wbf[slot][:, 128 * kc:128 * (kc + 1)],
                                  r_=hat.ap()[:, off + 512 * ci:off + 512 * (ci + 1)],
                                  st=(kc == 0), sp=(kc == 15):
                                  h.matmul(o, l, r_, start=st, stop=sp),
                                  reads=[r_wbf[slot], hresl[(off + 512 * ci) // 512]],
                                  writes=pres)
                        b2s = (lambda hh, oc: (eb[ebslot][:, 32 + oc:33 + oc]
                                               if hh else float(0.0)))(hp, oc)
                        em.op("dve", lambda h, ps=ps, b2s=b2s, ci=ci:
                              h.scalar_tensor_tensor(
                                  out=smT[ci % 2].ap()[:, 0:512], in0=ps,
                                  scalar=b2s,
                                  in1=GF[0].ap()[:, 512 * ci:512 * ci + 512],
                                  op0=ALU.add, op1=ALU.mult),
                              reads=pres + [r_eb[ebslot], r_GF[0][ci]],
                              writes=[r_smT[ci % 2]])
                        gt = 512 * ci
                        em.op("dve", lambda h, oc=oc, ci=ci, gt=gt,
                              o=x_res[oc].ap()[:, gt:gt + 512]:
                              h.tensor_tensor(out=o, in0=o,
                                              in1=smT[ci % 2].ap()[:, 0:512],
                                              op=ALU.add),
                              reads=[r_smT[ci % 2]] + x_slots(oc, gt, gt + 512),
                              writes=x_slots(oc, gt, gt + 512))

    ph_map = {"l0attn": phase_l0attn, "l0ffn": phase_l0ffn,
              "l1attn": phase_l1attn, "moe": phase_moe}
    for p in phases:
        ph_map[p]()

    for f in range(8):
        em.op("act", lambda h, f=f: h.activation(out=GF[1][:, :],
                                                 in_=x_res[f].ap()[:, :],
                                                 func=AF.Identity),
              reads=r_x[f], writes=r_GF[1])
        em.dma(yT_d[128 * f:128 * (f + 1), :], GF[1][:, :], reads=r_GF[1])

    with nc.Block() as block:
        em.build(block)
    return nc


def _prep_shared(inp):
    f32 = np.float32

    def a(k):
        return np.asarray(inp[k], f32)

    d = {}
    d["l0_wq"] = tile_lhsT(a("L0_attn_Wq"))
    d["l0_wk"] = tile_lhsT(a("L0_attn_Wk"))
    d["l0_wv"] = tile_lhsT(a("L0_attn_Wv"))
    wo4 = a("L0_attn_Wo").reshape(16, 64, 8, 128)
    d["l0_wo"] = np.ascontiguousarray(wo4.transpose(2, 1, 0, 3).reshape(8, 64, 2048))
    d["l0_bq"] = stripe(a("L0_attn_bq"))
    d["l0_bk"] = stripe(a("L0_attn_bk"))
    d["l0_bv"] = stripe(a("L0_attn_bv"))
    d["l0_bo"] = stripe(a("L0_attn_bo"))
    d["l0_w1"] = tile_lhsT(np.ascontiguousarray(a("L0_ffn_w1_w").T))
    d["l0_w2"] = tile_lhsT(np.ascontiguousarray(a("L0_ffn_w2_w").T))
    d["l0_h1"] = tile_lhsT(np.ascontiguousarray(a("L0_ffn_h1_w").T))
    d["l0_h2"] = tile_lhsT(np.ascontiguousarray(a("L0_ffn_h2_w").T))
    d["l0_w3"] = tile_lhsT(np.ascontiguousarray(a("L0_ffn_w3_w").T))
    d["l0_w1s"] = stripe(a("L0_ffn_w1_s"))
    d["l0_w2s"] = stripe(a("L0_ffn_w2_s"))
    d["l0_h1s"] = stripe(a("L0_ffn_h1_s"))
    d["l0_h2s"] = stripe(a("L0_ffn_h2_s"))
    d["l0_w3s"] = stripe(a("L0_ffn_w3_s"))
    d["l1_wqkv"] = tile_lhsT(a("L1_attn_w_qkv"))
    d["l1_proj"] = np.ascontiguousarray(a("L1_attn_proj"))
    d["l1_wout"] = tile_lhsT(a("L1_attn_w_out"))
    d["l1_bout"] = stripe(a("L1_attn_b_out"))
    d["rw1"] = tile_lhsT(a("L1_ffn_rw1"))
    d["rb1"] = stripe(a("L1_ffn_rb1"))
    d["rw2"] = tile_lhsT(a("L1_ffn_rw2"), mo=8)
    d["rb2"] = np.ascontiguousarray(a("L1_ffn_rb2").reshape(8, 1))
    d["w1e"] = np.stack([tile_lhsT(a("L1_ffn_w1")[e]) for e in range(8)])
    d["b1e"] = np.stack([stripe(a("L1_ffn_b1")[e]) for e in range(8)])
    d["w2e"] = np.stack([tile_lhsT(a("L1_ffn_w2")[e]) for e in range(8)])
    d["b2e"] = np.stack([stripe(a("L1_ffn_b2")[e]) for e in range(8)])
    return d


def kernel(**inputs):
    x = np.asarray(inputs["L0_attn_X_embed"], np.float32)
    shared = _prep_shared(inputs)
    nc = build_nc(_PHASES)
    in_maps = []
    for c in range(NCORES):
        m = dict(shared)
        m["xT"] = np.ascontiguousarray(x[c].T)
        in_maps.append(m)
    res = run_bass_kernel_spmd(nc, in_maps, core_ids=list(range(NCORES)))
    return np.stack([np.ascontiguousarray(r["yT"].T) for r in res.results])


# version 14
# speedup vs baseline: 1.0942x; 1.0942x over previous
import numpy as np
import ml_dtypes
from contextlib import ExitStack
import concourse.bass as bass
import concourse.mybir as mybir
from concourse.bass_utils import run_bass_kernel_spmd

F32 = mybir.dt.float32
BF16 = mybir.dt.bfloat16
AF = mybir.ActivationFunctionType
ALU = mybir.AluOpType
AX = mybir.AxisListType
BFNP = ml_dtypes.bfloat16

NTOK = 2048
D = 1024
NCORES = 8

_PHASES = ("l0attn", "l0ffn", "l1attn", "moe")
_L0CUT = "full"
_BAR = ""
_FFNCUT = "full"
_CASTENG = None
_DBG = False


class Res:
    __slots__ = ("w", "r")

    def __init__(self):
        self.w = None
        self.r = []


class Em:
    ENGS = ("pe", "act", "dve", "pool", "sp")

    def __init__(self, nc, ctx):
        self.nc = nc
        self.streams = {k: [] for k in self.ENGS}
        self.cnt = {k: 0 for k in self.ENGS}
        self.sat = {k: {} for k in self.ENGS}
        self.semh = {k: ctx.enter_context(nc.semaphore(f"sem_{k}")) for k in self.ENGS}

    def op(self, eng, fn, reads=(), writes=(), dma=False):
        evs = {}

        def need(ev):
            if ev is None:
                return
            s, v = ev
            if eng == "pe" and s == "pe":
                return
            if s in ("sp", "pool"):
                v = self.cnt[s]
            if self.sat[eng].get(s, 0) >= v:
                return
            if evs.get(s, 0) < v:
                evs[s] = v

        for r in reads:
            need(r.w)
        for w in writes:
            if w.r:
                for ev in w.r:
                    need(ev)
            else:
                need(w.w)
        inc = 16 if dma else 1
        self.cnt[eng] += inc
        ev = (eng, self.cnt[eng])
        waits = sorted(evs.items())
        self.streams[eng].append((waits, fn, eng, inc))
        for s, v in waits:
            self.sat[eng][s] = max(self.sat[eng].get(s, 0), v)
        for w in writes:
            w.w = ev
            w.r = []
        for r in reads:
            r.r.append(ev)
        return ev

    def barrier(self):
        snap = dict(self.cnt)
        for eng in self.ENGS:
            waits = sorted((s, v) for s, v in snap.items()
                           if s != eng and v > self.sat[eng].get(s, 0) and v > 0)
            self.cnt[eng] += 1
            self.streams[eng].append((waits, lambda h: h.nop(), eng, 1))
            for sv, vv in waits:
                self.sat[eng][sv] = max(self.sat[eng].get(sv, 0), vv)

    def dma(self, dst, src, reads=(), writes=(), q="sp", **kw):
        return self.op(q, lambda h: h.dma_start(out=dst, in_=src, **kw),
                       reads=reads, writes=writes, dma=True)

    def build(self, block):
        semh = self.semh
        finals = dict(self.cnt)

        def mk(name):
            stream = self.streams[name]

            def body(h):
                for waits, fn, seng, inc in stream:
                    for s, v in waits:
                        h.wait_ge(semh[s], v)
                    fn(h).then_inc(semh[seng], inc)
                if name == "sp":
                    for s, v in finals.items():
                        if v > 0:
                            h.wait_ge(semh[s], v)
            return body

        block.tensor(mk("pe"))
        block.scalar(mk("act"))
        block.vector(mk("dve"))
        block.gpsimd(mk("pool"))
        block.sync(mk("sp"))


def tile_lhsT(w_in_out, mo=128):
    d_in, d_out = w_in_out.shape
    n_kc, n_oc = d_in // 128, (d_out + mo - 1) // mo
    w4 = w_in_out.reshape(n_kc, 128, n_oc, mo)
    return np.ascontiguousarray(w4.transpose(2, 1, 0, 3).reshape(n_oc, 128, n_kc * mo))


def stripe(v, p=128):
    return np.ascontiguousarray(np.asarray(v).reshape(-1, p).T)


def build_nc(phases=_PHASES):
    nc = bass.Bass("TRN2", target_bir_lowering=False)
    ctx = ExitStack()

    def din(name, shape):
        return nc.dram_tensor(name, list(shape), F32, kind="ExternalInput").ap()

    xT_d = din("xT", (D, NTOK))
    wq_d = din("l0_wq", (8, 128, 1024))
    wk_d = din("l0_wk", (8, 128, 1024))
    wv_d = din("l0_wv", (8, 128, 1024))
    wo_d = din("l0_wo", (8, 64, 2048))
    bq_d = din("l0_bq", (128, 8))
    bk_d = din("l0_bk", (128, 8))
    bv_d = din("l0_bv", (128, 8))
    bo_d = din("l0_bo", (128, 8))
    w1_d = din("l0_w1", (32, 128, 1024))
    w2_d = din("l0_w2", (32, 128, 1024))
    h1_d = din("l0_h1", (32, 128, 4096))
    h2_d = din("l0_h2", (32, 128, 4096))
    w3_d = din("l0_w3", (8, 128, 4096))
    w1s_d = din("l0_w1s", (128, 32))
    w2s_d = din("l0_w2s", (128, 32))
    h1s_d = din("l0_h1s", (128, 32))
    h2s_d = din("l0_h2s", (128, 32))
    w3s_d = din("l0_w3s", (128, 8))
    qkv_d = din("l1_wqkv", (24, 128, 1024))
    proj_d = din("l1_proj", (128, 64))
    wout_d = din("l1_wout", (8, 128, 1024))
    bout_d = din("l1_bout", (128, 8))
    rw1_d = din("rw1", (32, 128, 1024))
    rb1_d = din("rb1", (128, 32))
    rw2_d = din("rw2", (1, 128, 256))
    rb2_d = din("rb2", (8, 1))
    w1e_d = din("w1e", (8, 32, 128, 1024))
    b1e_d = din("b1e", (8, 128, 32))
    w2e_d = din("w2e", (8, 8, 128, 4096))
    b2e_d = din("b2e", (8, 128, 8))
    yT_d = nc.dram_tensor("yT", [D, NTOK], F32, kind="ExternalOutput").ap()
    dbg_d = nc.dram_tensor("dbg", [128, 65536], BF16, kind="ExternalOutput").ap() \
        if _DBG else None

    ident_bf_d = nc.inline_tensor(np.eye(128, dtype=BFNP), name="ident_bf").ap()
    ident_f_d = nc.inline_tensor(np.eye(128, dtype=np.float32), name="ident_f").ap()
    sel8_np = np.zeros((8, 8 * 128), dtype=BFNP)
    for e in range(8):
        sel8_np[e, e * 128:(e + 1) * 128] = BFNP(1.0)
    sel8_d = nc.inline_tensor(sel8_np, name="sel8").ap()
    ones_d = nc.inline_tensor(np.ones((128, 128), dtype=BFNP), name="ones_c").ap()
    nones_d = nc.inline_tensor(np.full((128, 128), -1.0, dtype=BFNP),
                               name="nones_c").ap()

    A = nc.alloc_sbuf_tensor
    x_res = [A(f"xres{i}", [128, NTOK], BF16) for i in range(8)]
    xn = [A(f"xn{i}", [128, 1024], BF16) for i in range(8)]
    hatQ = A("hatQ", [128, 16384], BF16)
    hatK = A("hatK", [128, 16384], BF16)
    hatV = A("hatV", [128, 16384], BF16)
    GF = [A(f"gf{i}", [128, NTOK], F32) for i in range(2)]
    wf32 = [A(f"wf32_{i}", [128, 1024], F32) for i in range(2)]
    wbf = [A(f"wbf{i}", [128, 4096], BF16) for i in range(3)]
    Pbf = [A(f"pbf{i}", [128, NTOK], BF16) for i in range(2)]
    ones_bf = A("ones_bf", [128, 128], BF16)
    ident_bf = A("identb", [128, 128], BF16)
    ident_f = A("identf", [128, 128], F32)
    biases = A("biases", [128, 48], F32)
    scales = A("scales", [128, 136], F32)
    rb1_sb = A("rb1sb", [128, 32], F32)
    projbf = A("projbf", [128, 64], BF16)
    eb = [A(f"eb{i}", [128, 40], F32) for i in range(2)]
    smT = [A(f"smT{i}", [128, 512], BF16) for i in range(2)]
    cols = A("colsb", [128, 8], F32)
    cbias = A("cbias", [128, 4], F32)

    psA = nc.alloc_psum_tensor("psA", [128, 2048], F32)
    psB = nc.alloc_psum_tensor("psB", [128, 2048], F32)

    em = Em(nc, ctx)

    def RL(n):
        return [Res() for _ in range(n)]

    r_x = [RL(4) for _ in range(8)]
    r_xn = RL(8)
    r_hQ, r_hK, r_hV = RL(32), RL(32), RL(32)
    r_GF = [RL(4) for _ in range(2)]
    r_wf = RL(2)
    r_wbf = RL(3)
    r_P = RL(2)
    r_const = Res()
    r_bias = Res()
    r_scale = Res()
    r_rb1 = Res()
    r_proj = Res()
    r_eb = RL(2)
    r_smT = RL(2)
    r_cols = Res()
    r_colsP = [Res(), Res()]
    r_cbias = Res()
    r_rep = Res()
    r_wesb = RL(2)
    r_psA = RL(4)
    r_psB = RL(4)

    def slots(rlist, c0, c1):
        return rlist[c0 // 512: (c1 + 511) // 512]

    def psres(group, c0, c1):
        rl = r_psA if group is psA else r_psB
        return rl[c0 // 512: (c1 + 511) // 512]

    def x_slots(f, c0, c1):
        return r_x[f][c0 // 512: (c1 + 511) // 512]

    em.dma(ones_bf[:], ones_d, writes=[r_const])
    em.op("dve", lambda h: h.memset(cbias[:, 0:1], 1e-5), writes=[r_cbias])
    em.op("dve", lambda h: h.memset(cbias[:, 1:2], 1e-8), writes=[r_cbias])
    em.op("dve", lambda h: h.memset(cbias[:, 2:3], -1.0), writes=[r_cbias])
    em.op("dve", lambda h: h.memset(cbias[:, 3:4], 1.0), writes=[r_cbias])
    em.dma(ident_bf[:], ident_bf_d, writes=[r_const])
    em.dma(ident_f[:], ident_f_d, writes=[r_const])
    em.dma(biases[:, 0:8], bq_d, writes=[r_bias])
    em.dma(biases[:, 8:16], bk_d, writes=[r_bias])
    em.dma(biases[:, 16:24], bv_d, writes=[r_bias])
    em.dma(biases[:, 24:32], bo_d, writes=[r_bias])
    em.dma(biases[:, 32:40], bout_d, writes=[r_bias])
    em.dma(biases[0:8, 40:41], rb2_d, writes=[r_bias])
    em.op("dve", lambda h: h.tensor_scalar(out=biases[:, 0:16], in0=biases[:, 0:16],
                                           scalar1=float(1 / 32.0), scalar2=None,
                                           op0=ALU.mult),
          reads=[r_bias], writes=[r_bias])
    em.dma(scales[:, 0:32], w1s_d, writes=[r_scale])
    em.dma(scales[:, 32:64], w2s_d, writes=[r_scale])
    em.dma(scales[:, 64:96], h1s_d, writes=[r_scale])
    em.dma(scales[:, 96:128], h2s_d, writes=[r_scale])
    em.dma(scales[:, 128:136], w3s_d, writes=[r_scale])
    em.dma(rb1_sb[:], rb1_d, writes=[r_rb1])

    for f in range(8):
        g = f % 2
        em.dma(GF[g][:, :], xT_d[128 * f:128 * (f + 1), :], writes=r_GF[g],
               q=("sp", "pool")[g])
        em.op("pool", lambda h, d=x_res[f], g=g: h.tensor_copy(d[:], GF[g][:, :]),
              reads=r_GF[g], writes=r_x[f])

    wcache = {}
    wc_res = {}
    wc_loaded = set()
    _castrr = [0]

    _qrr = [0]

    def wq():
        _qrr[0] += 1
        return ("sp", "pool")[_qrr[0] % 2]

    def cast_op(dst, src, reads, writes):
        eng = _CASTENG or "act"
        _castrr[0] += 1
        if eng == "act":
            em.op("act", lambda h, d=dst, s=src: h.activation(
                out=d, in_=s, func=AF.Identity), reads=reads, writes=writes)
        else:
            em.op(eng, lambda h, d=dst, s=src: h.tensor_copy(d, s),
                  reads=reads, writes=writes)

    def wc_row(name, row, ncols, nrows):
        if name not in wcache:
            wcache[name] = nc.dram_tensor(f"wc_{name}", [nrows, 128, ncols],
                                          BF16, kind="Internal").ap()
        if (name, row) not in wc_res:
            wc_res[(name, row)] = Res()
        return wcache[name][row], wc_res[(name, row)]

    def load_lhsT(dram_oc_ap, ncols, slot, ckey=None, off=0):
        if "nocache" in _FFNCUT:
            ckey = None
        if ckey is not None:
            name, row, nrows = ckey
            cap, cres = wc_row(name, row, ncols, nrows)
        ql = wq()
        if ckey is not None and (ckey[0], ckey[1]) in wc_loaded:
            name, row, nrows = ckey
            cap, cres = wc_row(name, row, ncols, nrows)
            em.dma(wbf[slot][:, off:off + ncols], cap,
                   reads=[cres], writes=[r_wbf[slot]], q=ql)
            return
        for c0 in range(0, ncols, 1024):
            w = min(1024, ncols - c0)
            fs = _castrr[0] % 2
            em.dma(wf32[fs][:, 0:w], dram_oc_ap[:, c0:c0 + w], writes=[r_wf[fs]],
                   q=ql)
            cast_op(wbf[slot][:, off + c0:off + c0 + w], wf32[fs][:, 0:w],
                    [r_wf[fs]], [r_wbf[slot]])
        if ckey is not None:
            name, row, nrows = ckey
            cap, cres = wc_row(name, row, ncols, nrows)
            em.dma(cap, wbf[slot][:, off:off + ncols],
                   reads=[r_wbf[slot]], writes=[cres], q=ql)
            wc_loaded.add((name, row))

    def emit_ln(gc0, gc1, dst=None):
        mdst = GF[0].ap()[:, 0:512]
        rdst = GF[0].ap()[:, 512:1024]
        tmp = GF[0].ap()[:, 1024:1536]
        atmp = GF[0].ap()[:, 1536:2048]
        mres, rres, tres, ares = [r_GF[0][0]], [r_GF[0][1]], [r_GF[0][2]], [r_GF[0][3]]
        c = gc0
        while c < gc1:
            w = min(512, gc1 - c)
            s1 = psB[:, 0:w]
            s2 = psB[:, 512:512 + w]
            for f in range(8):
                sres = x_slots(f, c, c + w)
                em.op("act", lambda h, d=smT[f % 2][:, 0:w],
                      s=x_res[f].ap()[:, c:c + w]:
                      h.activation(out=d, in_=s, func=AF.Square),
                      reads=sres, writes=[r_smT[f % 2]])
                em.op("pe", lambda h, o=s1, r_=x_res[f].ap()[:, c:c + w],
                      st=(f == 0), sp=(f == 7):
                      h.matmul(o, ones_bf[:], r_, start=st, stop=sp),
                      reads=[r_const] + sres, writes=psres(psB, 0, w))
                em.op("pe", lambda h, o=s2, r_=smT[f % 2][:, 0:w],
                      st=(f == 0), sp=(f == 7):
                      h.matmul(o, ones_bf[:], r_, start=st, stop=sp),
                      reads=[r_const, r_smT[f % 2]], writes=psres(psB, 512, 512 + w))
            em.op("dve", lambda h, w=w, s1=s1: h.tensor_scalar(
                out=mdst[:, 0:w], in0=s1, scalar1=float(1.0 / D),
                scalar2=None, op0=ALU.mult),
                reads=psres(psB, 0, w), writes=mres)
            em.op("dve", lambda h, w=w, s2=s2: h.tensor_scalar(
                out=tmp[:, 0:w], in0=s2, scalar1=float(1.0 / D),
                scalar2=None, op0=ALU.mult),
                reads=psres(psB, 512, 512 + w), writes=tres)
            em.op("dve", lambda h, w=w: h.tensor_tensor(
                out=rdst[:, 0:w], in0=mdst[:, 0:w], in1=mdst[:, 0:w], op=ALU.mult),
                reads=mres, writes=rres)
            em.op("dve", lambda h, w=w: h.tensor_tensor(
                out=tmp[:, 0:w], in0=tmp[:, 0:w], in1=rdst[:, 0:w], op=ALU.subtract),
                reads=tres + rres, writes=tres)
            em.op("act", lambda h, w=w: h.activation(
                out=tmp[:, 0:w], in_=tmp[:, 0:w], func=AF.Ln, bias=cbias[:, 0:1]),
                reads=tres + [r_cbias], writes=tres)
            em.op("act", lambda h, w=w: h.activation(
                out=rdst[:, 0:w], in_=tmp[:, 0:w], func=AF.Exp, scale=float(-0.5)),
                reads=tres, writes=rres)
            for f in range(8):
                sres = x_slots(f, c, c + w)
                em.op("dve", lambda h, w=w, f=f, c=c: h.tensor_tensor(
                    out=atmp[:, 0:w], in0=x_res[f].ap()[:, c:c + w], in1=mdst[:, 0:w],
                    op=ALU.subtract),
                    reads=sres + mres, writes=ares)
                if dst is None:
                    oap, ores = xn[f].ap()[:, c - gc0:c - gc0 + w], [r_xn[f]]
                else:
                    oap, ores = dst(f, c, w)
                em.op("dve", lambda h, w=w, oap=oap: h.tensor_tensor(
                    out=oap, in0=atmp[:, 0:w], in1=rdst[:, 0:w],
                    op=ALU.mult),
                    reads=ares + rres, writes=ores)
            if "ln" in _BAR:
                em.barrier()
            c += w

    def proj(dram, n_oc, n_kc, tok_chunks, rhs_fn, evac,
             pbanks=((psB, 0), (psB, 512), (psB, 1024), (psB, 1536)), ckey=None):
        ncols = n_kc * 128
        for oc in range(n_oc):
            slot = oc % 3
            load_lhsT(dram[oc], ncols, slot,
                      None if ckey is None else (ckey[0], ckey[1] + oc, ckey[2]))
            for ci, (c0, w) in enumerate(tok_chunks):
                grp, base = pbanks[(oc * len(tok_chunks) + ci) % len(pbanks)]
                ps = grp[:, base:base + w]
                pres = psres(grp, base, base + w)
                for kc in range(n_kc):
                    rhs, rreads = rhs_fn(kc, c0, w)
                    em.op("pe", lambda h, o=ps, l=wbf[slot][:, 128 * kc:128 * (kc + 1)],
                          r_=rhs, st=(kc == 0), sp=(kc == n_kc - 1):
                          h.matmul(o, l, r_, start=st, stop=sp),
                          reads=[r_wbf[slot]] + rreads, writes=pres)
                evac(oc, ci, c0, w, ps, pres)

    def xn_rhs_at(base):
        def f(kc, c0, w):
            return xn[kc].ap()[:, c0 - base:c0 - base + w], [r_xn[kc]]
        return f

    def phase_l0attn():
        for H in range(2):
            t0 = 1024 * H
            emit_ln(t0, t0 + 1024)
            tok_chunks = [(t0, 512), (t0 + 512, 512)]

            def mk_evac(hat, hres, bcol, scl):
                def cb(oc, ci, c0, w, ps, pres):
                    c4 = (c0 - t0) // 128
                    psv = ps.rearrange("p (jh jl) -> p jh jl", jl=128)
                    hat4 = hat.ap().rearrange("p (el oo t) -> p el oo t", oo=8, t=128)
                    dst = hat4[:, c4:c4 + 4, oc, :]
                    sl = slots(hres, 1024 * c4, 1024 * (c4 + 4))
                    em.op("act", lambda h, d=dst, s=psv, b=biases[:, bcol + oc:bcol + oc + 1],
                          sc=float(scl):
                          h.activation(out=d, in_=s, func=AF.Identity, bias=b, scale=sc),
                          reads=pres + [r_bias], writes=sl)
                return cb

            if _L0CUT == "ln":
                continue
            proj(wq_d, 8, 8, tok_chunks, xn_rhs_at(t0), mk_evac(hatQ, r_hQ, 0, 1 / 32.0), ckey=("wq", 0, 8))
            proj(wk_d, 8, 8, tok_chunks, xn_rhs_at(t0), mk_evac(hatK, r_hK, 8, 1 / 32.0), ckey=("wk", 0, 8))
            proj(wv_d, 8, 8, tok_chunks, xn_rhs_at(t0), mk_evac(hatV, r_hV, 16, 1.0), ckey=("wv", 0, 8))

            em.dma(hatV.ap()[0:64, 8192:16384], hatK.ap()[64:128, 0:8192],
                   reads=slots(r_hK, 0, 8192), writes=slots(r_hV, 8192, 16384),
                   q="pool")
            em.dma(hatV.ap()[64:128, 8192:16384], hatK.ap()[0:64, 0:8192],
                   reads=slots(r_hK, 0, 8192), writes=slots(r_hV, 8192, 16384),
                   q="pool")

            if _L0CUT == "qkv":
                continue
            for el in range(8):
                base = 1024 * el
                qsl = slots(r_hQ, base, base + 1024)
                ksl = slots(r_hK, base, base + 1024)
                vsl = slots(r_hV, base, base + 1024)
                def emit_accum(gb):
                    for j in range(4):
                        em.op("pe", lambda h, o=psB[:, 512 * j:512 * (j + 1)],
                              r_=Pbf[gb % 2].ap()[:, 512 * j:512 * (j + 1)],
                              st=(gb == 0), sp=(gb == 15):
                              h.matmul(o, ones_bf[:], r_, start=st, stop=sp),
                              reads=[r_const, r_P[gb % 2]], writes=[r_psB[j]])

                for gb in range(16):
                    shb, ocb = gb % 2, gb // 2
                    for j in range(4):
                        shj, cj = j // 2, j % 2
                        if shj == shb:
                            lt = hatK.ap()[64 * shj:64 * (shj + 1),
                                           base + 128 * ocb:base + 128 * (ocb + 1)]
                            lres = ksl
                        else:
                            lt = hatV.ap()[64 * shj:64 * (shj + 1),
                                           8192 + base + 128 * ocb:
                                           8192 + base + 128 * (ocb + 1)]
                            lres = slots(r_hV, 8192 + base, 8192 + base + 1024)
                        rhs = hatQ.ap()[64 * shj:64 * (shj + 1),
                                        base + 512 * cj:base + 512 * (cj + 1)]
                        em.op("pe", lambda h, o=psA[:, 512 * j:512 * (j + 1)], l=lt,
                              r_=rhs: h.matmul(o, l, r_, start=True, stop=True),
                              reads=lres + qsl, writes=[r_psA[j]])
                    if gb > 0:
                        emit_accum(gb - 1)
                    cp = 4 * (gb % 2)
                    rcp = r_colsP[gb % 2]
                    for hh in range(2):
                        em.op("act", lambda h, hh=hh, cp=cp, d=Pbf[gb % 2]:
                              h.activation(
                                  out=d[:, 1024 * hh:1024 * (hh + 1)],
                                  in_=psA[:, 1024 * hh:1024 * (hh + 1)], func=AF.Exp,
                                  bias=cbias[:, 2:3],
                                  accum_out=cols[:, cp + hh:cp + hh + 1]),
                              reads=[r_psA[2 * hh], r_psA[2 * hh + 1], r_cbias, rcp],
                              writes=[r_P[gb % 2], rcp])
                    em.op("dve", lambda h, cp=cp: h.tensor_tensor(
                        out=cols[:, cp + 2:cp + 3], in0=cols[:, cp:cp + 1],
                        in1=cols[:, cp + 1:cp + 2], op=ALU.add),
                        reads=[rcp], writes=[rcp])
                    em.op("dve", lambda h, cp=cp: h.reciprocal(
                        out=cols[:, cp + 3:cp + 4], in_=cols[:, cp + 2:cp + 3]),
                        reads=[rcp], writes=[rcp])
                    em.op("dve", lambda h, cp=cp, d=Pbf[gb % 2]: h.tensor_scalar(
                        out=d[:], in0=d[:], scalar1=cols[:, cp + 3:cp + 4],
                        scalar2=None, op0=ALU.mult),
                        reads=[r_P[gb % 2], rcp], writes=[r_P[gb % 2]])
                emit_accum(15)
                for sh in range(2):
                    em.op("dve", lambda h, sh=sh,
                          d=hatQ.ap()[64 * sh:64 * (sh + 1), base:base + 1024],
                          v=hatV.ap()[64 * sh:64 * (sh + 1), base:base + 1024],
                          r_=psB[64 * sh:64 * (sh + 1), 1024 * sh:1024 * (sh + 1)]:
                          h.tensor_tensor(out=d, in0=v, in1=r_, op=ALU.mult),
                          reads=vsl + r_psB, writes=qsl)

            if _L0CUT == "attn":
                continue
            em.dma(hatQ.ap()[0:64, 8192:16384], hatQ.ap()[64:128, 0:8192],
                   reads=slots(r_hQ, 0, 8192), writes=slots(r_hQ, 8192, 16384),
                   q="pool")
            hat4Q = hatQ.ap().rearrange("p (el oo t) -> p el oo t", oo=8, t=128)
            hat4Qd = hatQ.ap()[:, 8192:16384].rearrange(
                "p (el oo t) -> p el oo t", oo=8, t=128)
            for oc in range(8):
                slot = oc % 3
                ql = wq()
                if ("wo", oc) in wc_loaded:
                    cap, cres = wc_row("wo", oc, 2048, 8)
                    em.dma(wbf[slot][0:64, 0:2048], cap[0:64, :],
                           reads=[cres], writes=[r_wbf[slot]], q=ql)
                else:
                    for pc in range(2):
                        fs = _castrr[0] % 2
                        em.dma(wf32[fs][0:64, 0:1024],
                               wo_d[oc, :, 1024 * pc:1024 * (pc + 1)],
                               writes=[r_wf[fs]], q=ql)
                        cast_op(wbf[slot][0:64, 1024 * pc:1024 * (pc + 1)],
                                wf32[fs][0:64, 0:1024],
                                [r_wf[fs]], [r_wbf[slot]])
                    cap, cres = wc_row("wo", oc, 2048, 8)
                    em.dma(cap[0:64, :], wbf[slot][0:64, 0:2048],
                           reads=[r_wbf[slot]], writes=[cres], q=ql)
                    wc_loaded.add(("wo", oc))
                for ci in range(2):
                    bk = 512 * ((2 * oc + ci) % 4)
                    ps = psB[:, bk:bk + 512]
                    pres = psres(psB, bk, bk + 512)
                    el0 = 4 * ci
                    for kc in range(16):
                        lt = wbf[slot][0:64, 128 * kc:128 * (kc + 1)]
                        hv = hat4Q if kc % 2 == 0 else hat4Qd
                        rhs = hv[0:64, el0:el0 + 4, kc // 2, :]
                        rs = (slots(r_hQ, 1024 * el0, 1024 * (el0 + 4))
                              if kc % 2 == 0 else
                              slots(r_hQ, 8192 + 1024 * el0, 8192 + 1024 * (el0 + 4)))
                        em.op("pe", lambda h, o=ps, l=lt, r_=rhs, st=(kc == 0),
                              sp=(kc == 15): h.matmul(o, l, r_, start=st, stop=sp),
                              reads=[r_wbf[slot]] + rs, writes=pres)
                    gt = t0 + 512 * ci
                    if _L0CUT == "womm":
                        em.op("dve", lambda h, p=ps: h.tensor_copy(
                            GF[1][:, 0:512], p), reads=pres, writes=[r_GF[1][0]])
                    else:
                        em.op("dve", lambda h, o=x_res[oc].ap()[:, gt:gt + 512], p=ps,
                              b=biases[:, 24 + oc:25 + oc]:
                              h.scalar_tensor_tensor(out=o, in0=p, scalar=b, in1=o,
                                                     op0=ALU.add, op1=ALU.add),
                              reads=pres + [r_bias] + x_slots(oc, gt, gt + 512),
                              writes=x_slots(oc, gt, gt + 512))

    def phase_l0ffn():
        for tq in range(4):
            t0 = 512 * tq
            if "noln" not in _FFNCUT:
                emit_ln(t0, t0 + 512)

            for oc in range(32):
                slot = oc % 3
                load_lhsT(w1_d[oc], 1024, slot, ckey=("w1", oc, 32))
                bk1 = 512 * ((2 * oc) % 4)
                ps1 = psB[:, bk1:bk1 + 512]
                for kc in range(8):
                    if "noln" in _FFNCUT:
                        r_ap, r_rd = x_res[kc].ap()[:, t0:t0 + 512], x_slots(kc, t0, t0 + 512)
                    else:
                        r_ap, r_rd = xn[kc].ap()[:, 0:512], [r_xn[kc]]
                    em.op("pe", lambda h, o=ps1,
                          l=wbf[slot][:, 128 * kc:128 * (kc + 1)],
                          r_=r_ap, st=(kc == 0), sp=(kc == 7):
                          h.matmul(o, l, r_, start=st, stop=sp),
                          reads=[r_wbf[slot]] + r_rd, writes=psres(psB, bk1, bk1 + 512))
                if oc % 2 == 0:
                    tA, tB = smT[0][:, 0:512], smT[1][:, 0:512]
                    rA, rB = r_smT[0], r_smT[1]
                else:
                    tA, tB = Pbf[0].ap()[:, 0:512], Pbf[0].ap()[:, 512:1024]
                    rA, rB = r_P[0], r_P[0]
                em.op("act", lambda h, oc=oc, ps1=ps1, tA=tA: h.activation(
                    out=tA, in_=ps1, func=AF.Silu,
                    scale=scales[:, oc:oc + 1]),
                    reads=psres(psB, bk1, bk1 + 512) + [r_scale], writes=[rA])
                load_lhsT(w2_d[oc], 1024, slot, ckey=("w2", oc, 32), off=1024)
                bk2 = 512 * ((2 * oc + 1) % 4)
                ps2 = psB[:, bk2:bk2 + 512]
                for kc in range(8):
                    if "noln" in _FFNCUT:
                        r_ap, r_rd = x_res[kc].ap()[:, t0:t0 + 512], x_slots(kc, t0, t0 + 512)
                    else:
                        r_ap, r_rd = xn[kc].ap()[:, 0:512], [r_xn[kc]]
                    em.op("pe", lambda h, o=ps2,
                          l=wbf[slot][:, 1024 + 128 * kc:1024 + 128 * (kc + 1)],
                          r_=r_ap, st=(kc == 0), sp=(kc == 7):
                          h.matmul(o, l, r_, start=st, stop=sp),
                          reads=[r_wbf[slot]] + r_rd, writes=psres(psB, bk2, bk2 + 512))
                em.op("dve", lambda h, oc=oc, ps2=ps2, tB=tB: h.tensor_scalar(
                    out=tB, in0=ps2, scalar1=scales[:, 32 + oc:33 + oc],
                    scalar2=None, op0=ALU.mult),
                    reads=psres(psB, bk2, bk2 + 512) + [r_scale], writes=[rB])
                em.op("dve", lambda h, d=hatQ.ap()[:, 512 * oc:512 * (oc + 1)],
                      tA=tA, tB=tB:
                      h.tensor_tensor(out=d, in0=tA, in1=tB, op=ALU.mult),
                      reads=[rA, rB], writes=[r_hQ[oc]])
                if "ffn" in _BAR:
                    em.barrier()

            def hstage(dram, src_hat, src_res, dst_cb, n_oc, cname):
                for oc in range(n_oc):
                    slot = oc % 3
                    load_lhsT(dram[oc], 4096, slot, ckey=(cname, oc, n_oc))
                    bk = 512 * (oc % 4)
                    ps = psB[:, bk:bk + 512]
                    pres = psres(psB, bk, bk + 512)
                    for kc in range(32):
                        em.op("pe", lambda h, o=ps,
                              l=wbf[slot][:, 128 * kc:128 * (kc + 1)],
                              r_=src_hat.ap()[:, 512 * kc:512 * (kc + 1)],
                              st=(kc == 0), sp=(kc == 31):
                              h.matmul(o, l, r_, start=st, stop=sp),
                              reads=[r_wbf[slot], src_res[kc]], writes=pres)
                    dst_cb(oc, ps, pres)
                    if "ffn" in _BAR:
                        em.barrier()

            def gelu_to(dst_hat, dst_res, scol):
                def cb(oc, ps, pres):
                    em.op("act", lambda h, oc=oc, ps=ps,
                          d=dst_hat.ap()[:, 512 * oc:512 * (oc + 1)]:
                          h.activation(out=d, in_=ps, func=AF.Gelu,
                                       scale=scales[:, scol + oc:scol + oc + 1]),
                          reads=pres + [r_scale], writes=[dst_res[oc]])
                return cb

            if _DBG:
                em.dma(dbg_d[:, 16384 * tq:16384 * (tq + 1)], hatQ.ap()[:, :],
                       reads=list(r_hQ))
            if _FFNCUT != "w12":
                hstage(h1_d, hatQ, r_hQ, gelu_to(hatK, r_hK, 64), 32, "h1")
                if _FFNCUT != "h1":
                    hstage(h2_d, hatK, r_hK, gelu_to(hatQ, r_hQ, 96), 32, "h2")

            def w3_cb(oc, ps, pres):
                em.op("dve", lambda h, oc=oc, ps=ps,
                      o=x_res[oc].ap()[:, t0:t0 + 512]:
                      h.scalar_tensor_tensor(out=o, in0=ps,
                                             scalar=scales[:, 128 + oc:129 + oc],
                                             in1=o, op0=ALU.mult, op1=ALU.add),
                      reads=pres + [r_scale] + x_slots(oc, t0, t0 + 512),
                      writes=x_slots(oc, t0, t0 + 512))
            hstage(w3_d, hatQ, r_hQ, w3_cb, 8, "w3")

    def phase_l1attn():
        em.dma(GF[0][:, 0:64], proj_d, writes=r_GF[0])
        em.op("act", lambda h: h.activation(out=projbf[:], in_=GF[0][:, 0:64],
                                            func=AF.Identity),
              reads=r_GF[0], writes=[r_proj])

        def mk_evac(hat, hres):
            def cb(oc, ci, c0, w, ps, pres):
                d = hat.ap()[:, 2048 * oc + c0:2048 * oc + c0 + w]
                sl = slots(hres, 2048 * oc + c0, 2048 * oc + c0 + w)
                em.op("act", lambda h, d=d, ps=ps:
                      h.activation(out=d, in_=ps, func=AF.Identity),
                      reads=pres, writes=sl)
            return cb

        def phi(dst_off):
            for p in range(4):
                for sh in range(2):
                    hd = 2 * p + sh
                    src = hatQ.ap()[:, 2048 * hd:2048 * (hd + 1)]
                    sres = slots(r_hQ, 2048 * hd, 2048 * (hd + 1))
                    for c0 in range(0, 2048, 512):
                        em.op("pe", lambda h, o=psA[64 * sh:64 * (sh + 1), c0:c0 + 512],
                              r_=src[:, c0:c0 + 512], tp=(0, 64 * sh):
                              h.matmul(o, projbf[:, 0:64], r_, start=True, stop=True,
                                       tile_position=tp),
                              reads=[r_proj] + sres, writes=psres(psA, c0, c0 + 512))
                em.op("act", lambda h: h.activation(
                    out=GF[0][:, :], in_=psA[:], func=AF.Relu),
                    reads=r_psA, writes=r_GF[0])
                em.op("act", lambda h: h.activation(
                    out=GF[1][:, :], in_=psA[:], func=AF.Relu, scale=float(-1.0)),
                    reads=r_psA, writes=r_GF[1])
                em.op("act", lambda h: h.activation(
                    out=GF[1][:, :], in_=GF[1][:, :], func=AF.Exp,
                    scale=float(-1.0)),
                    reads=r_GF[1], writes=r_GF[1])
                dst = hatV.ap()[:, dst_off + 2048 * p:dst_off + 2048 * (p + 1)]
                dsl = slots(r_hV, dst_off + 2048 * p, dst_off + 2048 * (p + 1))
                em.op("dve", lambda h, d=dst: h.tensor_tensor(
                    out=d, in0=GF[0][:, :], in1=GF[1][:, :], op=ALU.add),
                    reads=r_GF[0] + r_GF[1], writes=dsl)

        def qkv_half(sec_d, hat, hres, row0):
            for H in range(2):
                t0 = 1024 * H
                emit_ln(t0, t0 + 1024)
                tok_chunks = [(t0, 512), (t0 + 512, 512)]
                proj(sec_d, 8, 8, tok_chunks, xn_rhs_at(t0), mk_evac(hat, hres),
                     ckey=("qkv", row0, 24))

        qkv_half(qkv_d, hatQ, r_hQ, 0)
        phi(0)
        qkv_half(qkv_d[8:16], hatQ, r_hQ, 8)
        phi(8192)
        qkv_half(qkv_d[16:24], hatK, r_hK, 16)

        for hd in range(8):
            p, sh = hd // 2, hd % 2
            pq = hatV.ap()[64 * sh:64 * (sh + 1), 2048 * p:2048 * (p + 1)]
            pk = hatV.ap()[64 * sh:64 * (sh + 1), 8192 + 2048 * p:8192 + 2048 * (p + 1)]
            pqres = slots(r_hV, 2048 * p, 2048 * (p + 1))
            pkres = slots(r_hV, 8192 + 2048 * p, 8192 + 2048 * (p + 1))
            vt = hatK.ap()[:, 2048 * hd:2048 * (hd + 1)]
            vres = slots(r_hK, 2048 * hd, 2048 * (hd + 1))
            em.op("pool", lambda h: h.memset(GF[0][:, :], 0.0), writes=r_GF[0])
            for di, dl in enumerate(range(-3, 4)):
                a = max(0, -dl)
                b = min(2048, 2048 - dl)
                prod = Pbf[0].ap()[0:64, :]
                em.op("dve", lambda h, a=a, b=b, dl=dl, prod=prod, pq=pq, pk=pk:
                      h.tensor_tensor(out=prod[:, a:b], in0=pq[:, a:b],
                                      in1=pk[:, a + dl:b + dl], op=ALU.mult),
                      reads=pqres + pkres, writes=[r_P[0]])
                for bank in range(4):
                    s0, s1 = max(a, 512 * bank), min(b, 512 * (bank + 1))
                    if s0 >= s1:
                        continue
                    em.op("pe", lambda h, s0=s0, s1=s1, prod=prod: h.matmul(
                        psA[:, s0:s1], ones_bf[0:64, :], prod[:, s0:s1],
                        start=True, stop=True),
                        reads=[r_const, r_P[0]], writes=[r_psA[bank]])
                    em.op("pe", lambda h, s0=s0, s1=s1, prod=prod, st=(di == 0),
                          sp=(di == 6):
                          h.matmul(psB[:, s0:s1], ones_bf[0:64, :], prod[:, s0:s1],
                                   start=st, stop=sp),
                          reads=[r_const, r_P[0]], writes=[r_psB[bank]])
                em.op("dve", lambda h, a=a, b=b, dl=dl, vt=vt:
                      h.scalar_tensor_tensor(out=Pbf[1].ap()[:, a:b],
                                             in0=vt[:, a + dl:b + dl],
                                             scalar=float(1.0),
                                             in1=psA[:, a:b],
                                             op0=ALU.mult, op1=ALU.mult),
                      reads=vres + psres(psA, a, b), writes=[r_P[1]])
                em.op("pool", lambda h, a=a, b=b: h.tensor_tensor(
                    out=GF[0][:, a:b], in0=GF[0][:, a:b], in1=Pbf[1].ap()[:, a:b],
                    op=ALU.add),
                    reads=[r_P[1]] + r_GF[0], writes=r_GF[0])
            em.op("act", lambda h: h.activation(
                out=GF[1][:, :], in_=psB[:], func=AF.Ln, bias=cbias[:, 1:2]),
                reads=r_psB + [r_cbias], writes=r_GF[1])
            em.op("act", lambda h: h.activation(
                out=GF[1][:, :], in_=GF[1][:, :], func=AF.Exp, scale=float(-1.0)),
                reads=r_GF[1], writes=r_GF[1])
            osl = slots(r_hQ, 2048 * hd, 2048 * (hd + 1))
            em.op("dve", lambda h, hd=hd: h.tensor_tensor(
                out=hatQ.ap()[:, 2048 * hd:2048 * (hd + 1)],
                in0=GF[0][:, :], in1=GF[1][:, :], op=ALU.mult),
                reads=r_GF[0] + r_GF[1], writes=osl)

        def out_rhs(kc, c0, w):
            return (hatQ.ap()[:, 2048 * kc + c0:2048 * kc + c0 + w],
                    slots(r_hQ, 2048 * kc + c0, 2048 * kc + c0 + w))

        def evac_out(oc, ci, c0, w, ps, pres):
            em.op("dve", lambda h, o=x_res[oc].ap()[:, c0:c0 + w], ps=ps,
                  b=biases[:, 32 + oc:33 + oc]:
                  h.scalar_tensor_tensor(out=o, in0=ps, scalar=b, in1=o,
                                         op0=ALU.add, op1=ALU.add),
                  reads=pres + [r_bias] + x_slots(oc, c0, c0 + w),
                  writes=x_slots(oc, c0, c0 + w))
        proj(wout_d, 8, 8, [(0, 512), (512, 512), (1024, 512), (1536, 512)],
             out_rhs, evac_out)

    def phase_moe():
        lgT = GF[1].ap()[0:8, 0:2048]
        em.dma(Pbf[1].ap()[0:8, 0:1024], sel8_d, writes=[r_P[1]])
        em.dma(wf32[0][:, 0:256], rw2_d[0, :, :], writes=[r_wf[0]])
        em.op("pool", lambda h: h.tensor_copy(wbf[0][:, 3840:4096], wf32[0][:, 0:256]),
              reads=[r_wf[0]], writes=[r_wbf[0]])
        for tc in range(4):
            t0 = 512 * tc
            emit_ln(t0, t0 + 512)
            for oc in range(32):
                slot = oc % 3
                load_lhsT(rw1_d[oc], 1024, slot, ckey=("rw1", oc, 32))
                bk = 512 * (oc % 4)
                ps = psB[:, bk:bk + 512]
                pres = psres(psB, bk, bk + 512)
                for kc in range(8):
                    em.op("pe", lambda h, o=ps,
                          l=wbf[slot][:, 128 * kc:128 * (kc + 1)],
                          r_=xn[kc].ap()[:, 0:512], st=(kc == 0), sp=(kc == 7):
                          h.matmul(o, l, r_, start=st, stop=sp),
                          reads=[r_wbf[slot], r_xn[kc]], writes=pres)
                em.op("act", lambda h, oc=oc, ps=ps,
                      d=hatV.ap()[:, 512 * oc:512 * (oc + 1)]:
                      h.activation(out=d, in_=ps, func=AF.Silu,
                                   bias=rb1_sb[:, oc:oc + 1]),
                      reads=pres + [r_rb1], writes=[r_hV[oc]])
            ps = psA[0:8, 0:512]
            for kc in range(32):
                em.op("pe", lambda h, ps=ps, l=wbf[0][:, 3840 + 8 * kc:3848 + 8 * kc],
                      r_=hatV.ap()[:, 512 * kc:512 * (kc + 1)],
                      st=(kc == 0), sp=(kc == 31):
                      h.matmul(ps, l, r_, start=st, stop=sp),
                      reads=[r_wbf[0], r_hV[kc]], writes=[r_psA[0]])
            em.op("act", lambda h, tc=tc, ps=ps: h.activation(
                out=lgT[:, 512 * tc:512 * (tc + 1)], in_=ps, func=AF.Identity,
                bias=biases[0:8, 40:41]),
                reads=[r_psA[0], r_bias], writes=[r_GF[1][tc]])

        weT = Pbf[0].ap()[0:8, 0:2048]
        r_weT = r_P[0]
        m1 = GF[0].ap()[:, 1280:1296]
        m2 = GF[0].ap()[:, 1296:1312]
        dcol = GF[0].ap()[:, 1312:1328]
        p1 = GF[0].ap()[:, 1328:1344]
        r_m = [r_GF[0][3]]
        eqs = GF[0]
        for ch in range(16):
            em.op("pe", lambda h, ch=ch: h.transpose(
                psA[:, 512:520], lgT[:, 128 * ch:128 * (ch + 1)], ident_f[0:8, 0:8]),
                reads=r_GF[1] + [r_const], writes=[r_psA[1]])
            em.op("dve", lambda h, ch=ch: h.tensor_copy(
                GF[0].ap()[:, 1024 + 16 * ch:1024 + 16 * ch + 8], psA[:, 512:520]),
                reads=[r_psA[1]], writes=[r_GF[0][2]])
        for ch in range(16):
            lgtm = GF[0].ap()[:, 1024 + 16 * ch:1024 + 16 * ch + 8]
            em.op("dve", lambda h, ch=ch, lgtm=lgtm: h.reduce_max(
                m1[:, ch:ch + 1], lgtm, axis=AX.X),
                reads=[r_GF[0][2]], writes=r_m)
            em.op("dve", lambda h, ch=ch, lgtm=lgtm: h.tensor_scalar(
                out=eqs.ap()[:, 16 * ch:16 * ch + 8], in0=lgtm,
                scalar1=m1[:, ch:ch + 1], scalar2=None, op0=ALU.is_equal),
                reads=[r_GF[0][2]] + r_m, writes=[r_GF[0][0]])
            em.op("dve", lambda h, ch=ch, lgtm=lgtm: h.scalar_tensor_tensor(
                out=eqs.ap()[:, 256 + 16 * ch:256 + 16 * ch + 8],
                in0=eqs.ap()[:, 16 * ch:16 * ch + 8], scalar=float(-1e30),
                in1=lgtm, op0=ALU.mult, op1=ALU.add),
                reads=[r_GF[0][0], r_GF[0][2]], writes=[r_GF[0][0]])
            em.op("dve", lambda h, ch=ch: h.reduce_max(
                m2[:, ch:ch + 1], eqs.ap()[:, 256 + 16 * ch:256 + 16 * ch + 8],
                axis=AX.X),
                reads=[r_GF[0][0]], writes=r_m)
            em.op("dve", lambda h, ch=ch: h.tensor_scalar(
                out=eqs.ap()[:, 512 + 16 * ch:512 + 16 * ch + 8],
                in0=eqs.ap()[:, 256 + 16 * ch:256 + 16 * ch + 8],
                scalar1=m2[:, ch:ch + 1], scalar2=None, op0=ALU.is_equal),
                reads=[r_GF[0][0]] + r_m, writes=[r_GF[0][1]])
            em.op("dve", lambda h, ch=ch: h.tensor_tensor(
                out=dcol[:, ch:ch + 1], in0=m1[:, ch:ch + 1], in1=m2[:, ch:ch + 1],
                op=ALU.subtract),
                reads=r_m, writes=r_m)
        em.op("act", lambda h: h.activation(out=p1, in_=dcol, func=AF.Sigmoid),
              reads=r_m, writes=r_m)
        for ch in range(16):
            em.op("dve", lambda h, ch=ch: h.tensor_scalar(
                out=eqs.ap()[:, 768 + 16 * ch:768 + 16 * ch + 8],
                in0=eqs.ap()[:, 512 + 16 * ch:512 + 16 * ch + 8],
                scalar1=p1[:, ch:ch + 1], scalar2=None, op0=ALU.mult),
                reads=[r_GF[0][1]] + r_m, writes=[r_GF[0][1]])
            em.op("dve", lambda h, ch=ch: h.tensor_scalar(
                out=eqs.ap()[:, 16 * ch:16 * ch + 8],
                in0=eqs.ap()[:, 16 * ch:16 * ch + 8],
                scalar1=p1[:, ch:ch + 1], scalar2=None, op0=ALU.mult),
                reads=[r_GF[0][0]] + r_m, writes=[r_GF[0][0]])
            em.op("dve", lambda h, ch=ch: h.tensor_tensor(
                out=eqs.ap()[:, 16 * ch:16 * ch + 8],
                in0=eqs.ap()[:, 16 * ch:16 * ch + 8],
                in1=eqs.ap()[:, 512 + 16 * ch:512 + 16 * ch + 8], op=ALU.add),
                reads=[r_GF[0][0], r_GF[0][1]], writes=[r_GF[0][0]])
            em.op("dve", lambda h, ch=ch: h.tensor_tensor(
                out=eqs.ap()[:, 16 * ch:16 * ch + 8],
                in0=eqs.ap()[:, 16 * ch:16 * ch + 8],
                in1=eqs.ap()[:, 768 + 16 * ch:768 + 16 * ch + 8], op=ALU.subtract),
                reads=[r_GF[0][0], r_GF[0][1]], writes=[r_GF[0][0]])
            em.op("act", lambda h, ch=ch: h.activation(
                out=GF[1].ap()[:, 16 * ch:16 * ch + 8],
                in_=eqs.ap()[:, 16 * ch:16 * ch + 8], func=AF.Identity),
                reads=[r_GF[0][0]], writes=[r_GF[1][0]])
            em.op("pe", lambda h, ch=ch: h.transpose(
                psA[0:8, 640:768], GF[1].ap()[:, 16 * ch:16 * ch + 8], ident_f[:]),
                reads=[r_GF[1][0], r_const], writes=[r_psA[1]])
            em.op("act", lambda h, ch=ch: h.activation(
                out=weT[:, 128 * ch:128 * (ch + 1)], in_=psA[0:8, 640:768],
                func=AF.Identity),
                reads=[r_psA[1]], writes=[r_weT])

        emit_ln(0, 2048,
                dst=lambda f, c, w: (hatV.ap()[:, 2048 * f + c:2048 * f + c + w],
                                     slots(r_hV, 2048 * f + c, 2048 * f + c + w)))
        for e in range(8):
            ebslot = e % 2
            em.dma(eb[ebslot][:, 0:32], b1e_d[e, :, :], writes=[r_eb[ebslot]])
            em.dma(eb[ebslot][:, 32:40], b2e_d[e, :, :], writes=[r_eb[ebslot]])
            for ci in range(4):
                pb = 1024 + 512 * (ci % 2)
                em.op("pe", lambda h, e=e, ci=ci, pb=pb,
                      o=psA[:, pb:pb + 512],
                      l=Pbf[1].ap()[0:8, 128 * e:128 * (e + 1)],
                      r_=weT[:, 512 * ci:512 * (ci + 1)]:
                      h.matmul(o, l, r_, start=True, stop=True),
                      reads=[r_P[1], r_weT],
                      writes=psres(psA, pb, pb + 512))
                em.op("act", lambda h, ci=ci, pb=pb,
                      o=GF[0].ap()[:, 512 * ci:512 * ci + 512],
                      s=psA[:, pb:pb + 512]:
                      h.activation(out=o, in_=s, func=AF.Identity),
                      reads=psres(psA, pb, pb + 512),
                      writes=[r_GF[0][ci]])
            for hp in range(2):
                for oh in range(16):
                    oc = 16 * hp + oh
                    slot = oh % 3
                    load_lhsT(w1e_d[e, oc], 1024, slot)
                    hat = hatQ if oh < 8 else hatK
                    hresl = r_hQ if oh < 8 else r_hK
                    off = 2048 * (oh % 8)
                    for ci in range(4):
                        bk = 512 * (ci % 4)
                        ps = psB[:, bk:bk + 512]
                        pres = psres(psB, bk, bk + 512)
                        for kc in range(8):
                            xno = 2048 * kc + 512 * ci
                            em.op("pe", lambda h, o=ps,
                                  l=wbf[slot][:, 128 * kc:128 * (kc + 1)],
                                  r_=hatV.ap()[:, xno:xno + 512],
                                  st=(kc == 0), sp=(kc == 7):
                                  h.matmul(o, l, r_, start=st, stop=sp),
                                  reads=[r_wbf[slot], r_hV[xno // 512]],
                                  writes=pres)
                        em.op("act", lambda h, oc=oc, ps=ps, ebslot=ebslot,
                              d=hat.ap()[:, off + 512 * ci:off + 512 * (ci + 1)]:
                              h.activation(out=d, in_=ps, func=AF.Silu,
                                           bias=eb[ebslot][:, oc:oc + 1]),
                              reads=pres + [r_eb[ebslot]],
                              writes=[hresl[(off + 512 * ci) // 512]])
                for oc in range(8):
                    slot = oc % 3
                    load_lhsT(w2e_d[e, oc][:, 2048 * hp:2048 * (hp + 1)], 2048, slot)
                    for ci in range(4):
                        bk = 512 * (ci % 4)
                        ps = psB[:, bk:bk + 512]
                        pres = psres(psB, bk, bk + 512)
                        for kc in range(16):
                            hat = hatQ if kc < 8 else hatK
                            hresl = r_hQ if kc < 8 else r_hK
                            off = 2048 * (kc % 8)
                            em.op("pe", lambda h, o=ps,
                                  l=wbf[slot][:, 128 * kc:128 * (kc + 1)],
                                  r_=hat.ap()[:, off + 512 * ci:off + 512 * (ci + 1)],
                                  st=(kc == 0), sp=(kc == 15):
                                  h.matmul(o, l, r_, start=st, stop=sp),
                                  reads=[r_wbf[slot], hresl[(off + 512 * ci) // 512]],
                                  writes=pres)
                        b2s = (lambda hh, oc: (eb[ebslot][:, 32 + oc:33 + oc]
                                               if hh else float(0.0)))(hp, oc)
                        em.op("dve", lambda h, ps=ps, b2s=b2s, ci=ci:
                              h.scalar_tensor_tensor(
                                  out=smT[ci % 2].ap()[:, 0:512], in0=ps,
                                  scalar=b2s,
                                  in1=GF[0].ap()[:, 512 * ci:512 * ci + 512],
                                  op0=ALU.add, op1=ALU.mult),
                              reads=pres + [r_eb[ebslot], r_GF[0][ci]],
                              writes=[r_smT[ci % 2]])
                        gt = 512 * ci
                        em.op("dve", lambda h, oc=oc, ci=ci, gt=gt,
                              o=x_res[oc].ap()[:, gt:gt + 512]:
                              h.tensor_tensor(out=o, in0=o,
                                              in1=smT[ci % 2].ap()[:, 0:512],
                                              op=ALU.add),
                              reads=[r_smT[ci % 2]] + x_slots(oc, gt, gt + 512),
                              writes=x_slots(oc, gt, gt + 512))

    ph_map = {"l0attn": phase_l0attn, "l0ffn": phase_l0ffn,
              "l1attn": phase_l1attn, "moe": phase_moe}
    for p in phases:
        ph_map[p]()

    for f in range(8):
        em.op("act", lambda h, f=f: h.activation(out=GF[1][:, :],
                                                 in_=x_res[f].ap()[:, :],
                                                 func=AF.Identity),
              reads=r_x[f], writes=r_GF[1])
        em.dma(yT_d[128 * f:128 * (f + 1), :], GF[1][:, :], reads=r_GF[1])

    with nc.Block() as block:
        em.build(block)
    return nc


def _prep_shared(inp):
    f32 = np.float32

    def a(k):
        return np.asarray(inp[k], f32)

    d = {}
    d["l0_wq"] = tile_lhsT(a("L0_attn_Wq"))
    d["l0_wk"] = tile_lhsT(a("L0_attn_Wk"))
    d["l0_wv"] = tile_lhsT(a("L0_attn_Wv"))
    wo4 = a("L0_attn_Wo").reshape(16, 64, 8, 128)
    d["l0_wo"] = np.ascontiguousarray(wo4.transpose(2, 1, 0, 3).reshape(8, 64, 2048))
    d["l0_bq"] = stripe(a("L0_attn_bq"))
    d["l0_bk"] = stripe(a("L0_attn_bk"))
    d["l0_bv"] = stripe(a("L0_attn_bv"))
    d["l0_bo"] = stripe(a("L0_attn_bo"))
    d["l0_w1"] = tile_lhsT(np.ascontiguousarray(a("L0_ffn_w1_w").T))
    d["l0_w2"] = tile_lhsT(np.ascontiguousarray(a("L0_ffn_w2_w").T))
    d["l0_h1"] = tile_lhsT(np.ascontiguousarray(a("L0_ffn_h1_w").T))
    d["l0_h2"] = tile_lhsT(np.ascontiguousarray(a("L0_ffn_h2_w").T))
    d["l0_w3"] = tile_lhsT(np.ascontiguousarray(a("L0_ffn_w3_w").T))
    d["l0_w1s"] = stripe(a("L0_ffn_w1_s"))
    d["l0_w2s"] = stripe(a("L0_ffn_w2_s"))
    d["l0_h1s"] = stripe(a("L0_ffn_h1_s"))
    d["l0_h2s"] = stripe(a("L0_ffn_h2_s"))
    d["l0_w3s"] = stripe(a("L0_ffn_w3_s"))
    d["l1_wqkv"] = tile_lhsT(a("L1_attn_w_qkv"))
    d["l1_proj"] = np.ascontiguousarray(a("L1_attn_proj"))
    d["l1_wout"] = tile_lhsT(a("L1_attn_w_out"))
    d["l1_bout"] = stripe(a("L1_attn_b_out"))
    d["rw1"] = tile_lhsT(a("L1_ffn_rw1"))
    d["rb1"] = stripe(a("L1_ffn_rb1"))
    d["rw2"] = tile_lhsT(a("L1_ffn_rw2"), mo=8)
    d["rb2"] = np.ascontiguousarray(a("L1_ffn_rb2").reshape(8, 1))
    d["w1e"] = np.stack([tile_lhsT(a("L1_ffn_w1")[e]) for e in range(8)])
    d["b1e"] = np.stack([stripe(a("L1_ffn_b1")[e]) for e in range(8)])
    d["w2e"] = np.stack([tile_lhsT(a("L1_ffn_w2")[e]) for e in range(8)])
    d["b2e"] = np.stack([stripe(a("L1_ffn_b2")[e]) for e in range(8)])
    return d


def kernel(**inputs):
    x = np.asarray(inputs["L0_attn_X_embed"], np.float32)
    shared = _prep_shared(inputs)
    nc = build_nc(_PHASES)
    in_maps = []
    for c in range(NCORES):
        m = dict(shared)
        m["xT"] = np.ascontiguousarray(x[c].T)
        in_maps.append(m)
    res = run_bass_kernel_spmd(nc, in_maps, core_ids=list(range(NCORES)))
    return np.stack([np.ascontiguousarray(r["yT"].T) for r in res.results])
